# version 01 (final)
# baseline (speedup 1.0000x reference)
import math
from contextlib import ExitStack

import numpy as np
import ml_dtypes

import concourse.bass as bass
import concourse.mybir as mybir
from concourse.bass_utils import run_bass_kernel_spmd

F32 = mybir.dt.float32
BF16 = mybir.dt.bfloat16
AF = mybir.ActivationFunctionType
ALU = mybir.AluOpType
AX = mybir.AxisListType
BF = ml_dtypes.bfloat16

ENGINES = ("sync", "tensor", "vector", "scalar", "gpsimd")
T = 2048
TT = 512
NTT = T // TT
NCH = 8
D = 1024
GROUPS = [[0, 1, 2, 3], [4, 5, 6, 7]]
SE = 2048
NSLOT = 6



class Sem:
    def __init__(self, h):
        self.h = h
        self.n = 0


class P:
    def __init__(self, nc, ctx):
        self.nc = nc
        self.ctx = ctx
        self.ops = {e: [] for e in ENGINES}
        self.tiles = {}
        self.sems = {}
        self.dram = {}
        self._bar = self.sem("bar")

    def sem(self, name):
        if name not in self.sems:
            self.sems[name] = Sem(self.ctx.enter_context(self.nc.semaphore(name)))
        return self.sems[name]

    def tile(self, name, shape, dtype):
        if name not in self.tiles:
            self.tiles[name] = self.ctx.enter_context(
                self.nc.sbuf_tensor(name, shape, dtype))
        return self.tiles[name]

    def psum(self, name, shape, dtype=F32):
        if name not in self.tiles:
            self.tiles[name] = self.ctx.enter_context(
                self.nc.psum_tensor(name, shape, dtype))
        return self.tiles[name]

    def idram(self, name, shape, dtype=F32, addr_space="Local"):
        if name not in self.dram:
            self.dram[name] = self.nc.dram_tensor(
                name, shape, dtype, kind="Internal", addr_space=addr_space).ap()
        return self.dram[name]

    def emit(self, eng, fn, waits=(), inc=None, k=1):
        wl = tuple((s.h, v) for (s, v) in waits if v > 0)
        if inc is not None:
            inc.n += k
            it = (inc.h, k)
        else:
            it = None

        def cl(engine, fn=fn, wl=wl, it=it):
            for h, v in wl:
                engine.wait_ge(h, v)
            r = fn(engine)
            if it is not None:
                r.then_inc(*it)
        self.ops[eng].append(cl)
        return inc.n if inc is not None else None

    def wait(self, eng, sem, val=None):
        v = sem.n if val is None else val
        if v <= 0:
            return
        def cl(engine, h=sem.h, v=v):
            engine.wait_ge(h, v)
        self.ops[eng].append(cl)

    def pool_reset(self):
        self._pf = 0
        self._pb = 0

    def pallocF(self, rows, cols):
        v = self.poolF[0:rows, self._pf:self._pf + cols]
        self._pf += cols
        assert self._pf <= 512, "poolF overflow"
        return v

    def pallocB(self, rows, cols):
        v = self.poolB[0:rows, self._pb:self._pb + cols]
        self._pb += cols
        assert self._pb <= 2304, "poolB overflow"
        return v

    def barrier(self, engines=ENGINES):
        b = self._bar
        b.n += len(engines)
        target = b.n
        for e in engines:
            def cl(engine, t=target, h=b.h, drain=(e != "sync")):
                if drain:
                    engine.drain()
                engine.sem_inc(h, 1)
                engine.wait_ge(h, t)
            self.ops[e].append(cl)

    def run(self):
        with self.nc.Block() as block:
            for name in ENGINES:
                def body(engine, name=name):
                    for f in self.ops[name]:
                        f(engine)
                getattr(block, name)(body)



def setup_common(p):
    nc = p.nc
    p.x8 = [p.tile(f"x{c}", [128, T], F32) for c in range(NCH)]
    p.lnx8 = [p.tile(f"lnx{c}", [128, T], BF16) for c in range(NCH)]
    p.bigA = p.tile("bigA", [128, 32 * TT], BF16)
    p.bigB = p.tile("bigB", [128, 32 * TT], BF16)
    p.wring = p.tile("wring", [128, NSLOT * SE], BF16)
    p.scr0 = p.tile("scr0", [128, TT], F32)
    p.scr1 = p.tile("scr1", [128, TT], F32)
    p.rowA = p.tile("rowA", [1, TT], F32)
    p.rowB = p.tile("rowB", [1, TT], F32)
    p.rowAb = p.tile("rowAb", [1, TT], BF16)
    p.rowBb = p.tile("rowBb", [1, TT], BF16)
    p.poolF = p.tile("poolF", [128, 512], F32)
    p.scrE = p.tile("scrE", [128, TT], F32)
    p.poolB = p.tile("poolB", [128, 2304], BF16)
    p.pool_reset()
    p.sqt = [p.tile(f"sqt{i}", [128, TT], BF16) for i in range(2)]
    p.ones_col = p.tile("ones_col", [128, 1], BF16)
    p.ones_row = p.tile("ones_row", [1, 128], BF16)
    p.ident = p.tile("ident", [128, 128], BF16)
    ident_d = p.nc.inline_tensor(np.eye(128, dtype=BF), "ident_d").ap()
    p.identf = p.tile("identf", [128, 128], F32)
    identf_d = p.nc.inline_tensor(np.eye(128, dtype=np.float32), "identf_d").ap()
    p.selb = p.tile("selb", [8, 1024], BF16)
    sel_np = np.zeros((8, 1024), dtype=BF)
    for e_ in range(8):
        sel_np[e_, 128 * e_:128 * e_ + 128] = 1
    selb_d = p.nc.inline_tensor(sel_np, "selb_d").ap()
    p.pb = [p.psum(f"pb{i}", [128, 512], F32) for i in range(6)]
    p.pt = p.psum("pt", [128, 1024], BF16)

    ld = p.sem("init_ld")
    p.emit("gpsimd", lambda g: g.memset(p.ones_col[:, :], 1.0), inc=ld)
    p.emit("gpsimd", lambda g: g.memset(p.ones_row[:, :], 1.0), inc=ld)
    p.emit("sync", lambda s: s.dma_start(out=p.ident[:, :], in_=ident_d), inc=ld, k=16)
    p.emit("sync", lambda s: s.dma_start(out=p.selb[:, :], in_=selb_d), inc=ld, k=16)
    p.emit("sync", lambda s: s.dma_start(out=p.identf[:, :], in_=identf_d), inc=ld, k=16)
    consts = [0.0, 1e-5, 1e-6]
    p.cpool = p.tile("cpool", [128, len(consts)], F32)
    for i, v in enumerate(consts):
        p.emit("gpsimd", lambda g, i=i, v=v: g.memset(p.cpool[:, i:i + 1], v), inc=ld)
        nc.const_aps.register(F32, v, p.cpool[:, i:i + 1])
    p.init_ld = ld
    for e in ENGINES:
        p.wait(e, ld)


def load_x(p, x_dram):
    s = p.sem("xin")
    for c in range(NCH):
        p.emit("sync", lambda e, c=c: e.dma_start(
            out=p.x8[c][:, :], in_=x_dram[128 * c:128 * c + 128, :]), inc=s, k=16)
    for e in ENGINES:
        p.wait(e, s)


def store_x(p, x_out):
    s = p.sem("xout")
    for c in range(NCH):
        p.emit("sync", lambda e, c=c: e.dma_start(
            out=x_out[128 * c:128 * c + 128, :], in_=p.x8[c][:, :]), inc=s, k=16)
    p.wait("sync", s)



def emit_ln(p, tag, eps=1e-5, src=None, dst=None, gain_col=None, bias_col=None,
            src_bf=None):
    nc = p.nc
    src = src or p.x8
    dst = dst or p.lnx8
    sv = p.sem(f"{tag}_v")
    sa = p.sem(f"{tag}_a")
    sp = p.sem(f"{tag}_p")

    if src_bf is None:
        src_bf = dst
        for c in range(NCH):
            p.emit("gpsimd", lambda g, c=c: g.tensor_copy(dst[c][:, :], src[c][:, :]),
                   inc=sv)
    cast_done = sv.n

    for tt in range(NTT):
        sl = slice(TT * tt, TT * (tt + 1))
        psS, psQ, psM, psR = p.pb[0], p.pb[1], p.pb[2], p.pb[3]
        for c in range(NCH):
            p.emit("tensor", lambda t, c=c, sl=sl, psS=psS: t.matmul(
                psS[0:1, :], p.ones_col[:, :], src_bf[c][:, sl],
                start=(c == 0), stop=(c == NCH - 1)),
                waits=[(sv, cast_done), (p.init_ld, p.init_ld.n)]
                      if (c == 0 and tt == 0) else (),
                inc=sp if c == NCH - 1 else None)
        sp_q = sp.n
        sa_q = sa.n
        for c in range(NCH):
            buf = p.sqt[c % 2]
            w = [(sp, sp_q + c - 1 if c >= 2 else sp_q - 1)]
            p.emit("scalar", lambda e, c=c, sl=sl, buf=buf: e.activation(
                buf[:, :], src[c][:, sl], AF.Square), waits=w, inc=sa)
            p.emit("tensor", lambda t, c=c, buf=buf, psQ=psQ: t.matmul(
                psQ[0:1, :], p.ones_col[:, :], buf[:, :],
                start=(c == 0), stop=(c == NCH - 1)),
                waits=[(sa, sa_q + c + 1)], inc=sp)
        p.emit("scalar", lambda e, psS=psS: e.mul(
            p.rowA[0:1, :], psS[0:1, :], 1.0 / D), waits=[(sp, sp.n)], inc=sa)
        p.emit("scalar", lambda e, psQ=psQ: e.mul(
            p.rowB[0:1, :], psQ[0:1, :], 1.0 / D), inc=sa)
        p.emit("vector", lambda v: v.tensor_mul(
            p.scr0[0:1, :], p.rowA[0:1, :], p.rowA[0:1, :]),
            waits=[(sa, sa.n)], inc=sv)
        p.emit("vector", lambda v: v.tensor_sub(
            p.rowB[0:1, :], p.rowB[0:1, :], p.scr0[0:1, :]), inc=sv)
        p.emit("vector", lambda v: v.tensor_copy(
            p.rowAb[0:1, :], p.rowA[0:1, :]), inc=sv)
        p.emit("scalar", lambda e: e.activation(
            p.rowB[0:1, :], p.rowB[0:1, :], AF.Ln, bias=eps),
            waits=[(sv, sv.n)], inc=sa)
        p.emit("scalar", lambda e: e.activation(
            p.rowB[0:1, :], p.rowB[0:1, :], AF.Exp, scale=-0.5), inc=sa)
        p.emit("vector", lambda v: v.tensor_copy(
            p.rowBb[0:1, :], p.rowB[0:1, :]), waits=[(sa, sa.n)], inc=sv)
        p.emit("tensor", lambda t, psM=psM: t.matmul(
            psM[:, :], p.ones_row[:, :], p.rowAb[0:1, :], start=True, stop=True),
            waits=[(sv, sv.n)], inc=sp)
        p.emit("tensor", lambda t, psR=psR: t.matmul(
            psR[:, :], p.ones_row[:, :], p.rowBb[0:1, :], start=True, stop=True),
            inc=sp)
        bc_done = sp.n
        for c in range(NCH):
            scr = p.scr0
            p.emit("vector", lambda v, c=c, sl=sl, scr=scr, psM=psM: v.tensor_sub(
                scr[:, :], src[c][:, sl], psM[:, :]),
                waits=[(sp, bc_done)], inc=sv)
            if gain_col is None:
                p.emit("vector", lambda v, c=c, sl=sl, scr=scr, psR=psR: v.tensor_mul(
                    dst[c][:, sl], scr[:, :], psR[:, :]), inc=sv)
            else:
                p.emit("vector", lambda v, c=c, sl=sl, scr=scr, psR=psR: v.tensor_mul(
                    scr[:, :], scr[:, :], psR[:, :]), inc=sv)
                p.emit("vector", lambda v, c=c, sl=sl, scr=scr: v.tensor_scalar(
                    dst[c][:, sl], scr[:, :], gain_col[c], bias_col[c],
                    ALU.mult, ALU.add), inc=sv)
        if tt < NTT - 1:
            p.wait("tensor", sv)
            p.wait("scalar", sv)
    p.barrier()



class WRing:

    def __init__(self, p, tag):
        self.p = p
        self.tile = p.wring
        self.ld = p.sem(f"{tag}_ld")
        self.cur = 0
        self.g = 0
        self.slot_free = [None] * NSLOT
        self.ginfo = {}

    def can_load(self, nslots):
        cur = self.cur
        if cur + nslots > NSLOT:
            cur = 0
        return all(self.slot_free[x] is not False
                   for x in range(cur, cur + nslots))

    def load(self, src2d, nslots):
        cols = src2d.shape[1]
        assert cols <= nslots * SE
        if self.cur + nslots > NSLOT:
            self.cur = 0
        s0 = self.cur
        waits = {}
        for s in range(s0, s0 + nslots):
            fw = self.slot_free[s]
            assert fw is not None or fw is None
            if fw is False:
                raise RuntimeError(f"slot {s} reloaded before mark_free")
            if isinstance(fw, tuple):
                sem, val = fw
                waits[sem] = max(waits.get(sem, 0), val)
            self.slot_free[s] = False
        dst = self.tile[:, s0 * SE:s0 * SE + cols]
        self.p.emit("gpsimd", lambda g_, dst=dst, src=src2d: g_.dma_start(
            out=dst, in_=src), waits=list(waits.items()), inc=self.ld, k=16)
        g = self.g
        self.ginfo[g] = (s0, nslots)
        self.cur = s0 + nslots
        self.g += 1
        return g

    def wv(self, g):
        return 16 * (g + 1)

    def view(self, g, off, n):
        s0, _ = self.ginfo[g]
        return self.tile[:, s0 * SE + off:s0 * SE + off + n]

    def mark_free(self, g, sem, val):
        s0, ns = self.ginfo[g]
        for s in range(s0, s0 + ns):
            self.slot_free[s] = (sem, val)


def load_small(p, dram_ap, rows, cols, dtype=F32):
    if dtype == F32:
        tl = p.pallocF(rows, cols)
        p.emit("sync", lambda e: e.dma_start(out=tl, in_=dram_ap),
               inc=p.sem("smld"), k=16)
    else:
        tl = p.pallocB(rows, cols)
        p.emit("gpsimd", lambda e: e.dma_start(out=tl, in_=dram_ap),
               inc=p.sem("smldg"), k=16)
    return tl


def wait_small(p, engines=("tensor", "scalar", "vector")):
    for e in engines:
        p.wait(e, p.sem("smld"))
        p.wait(e, p.sem("smldg"))


def chunked(dram_ap, p=128):
    return dram_ap.rearrange("(kc p) m -> p kc m", p=p)


def host_blocks(Wt, mblk):
    Wt = np.asarray(Wt, np.float32)
    K_, M_ = Wt.shape
    kc = K_ // 128
    nb = M_ // mblk
    return np.ascontiguousarray(
        Wt.reshape(kc, 128, nb, mblk).transpose(2, 1, 0, 3).reshape(nb, 128, kc * mblk))



def emit_l0_ffn(p, W, dbg=None):
    nc = p.nc
    emit_ln(p, "l0f_ln")

    p.pool_reset()
    s1 = load_small(p, W["s1"], 128, 32)
    s2 = load_small(p, W["s2"], 128, 32)
    sh1 = load_small(p, W["sh1"], 128, 32)
    sh2 = load_small(p, W["sh2"], 128, 32)
    s3 = load_small(p, W["s3"], 128, 8)

    ring = WRing(p, "l0f")
    sp = p.sem("l0f_p")
    sa = p.sem("l0f_a")
    sv = p.sem("l0f_v")

    A = [p.bigA[:, i * TT:(i + 1) * TT] for i in range(32)]
    B = [p.bigB[:, i * TT:(i + 1) * TT] for i in range(32)]
    tpool = p.pb

    w1c, w2c, h1c, h2c, w3c = W["w1B"], W["w2B"], W["h1B"], W["h2B"], W["w3B"]
    wait_small(p)

    for tt in range(NTT):
        sl = slice(TT * tt, TT * (tt + 1))
        sv1 = sv.n
        g1 = {}
        g2 = {}
        seq = [(w, k) for k in range(8) for w in (1, 2)]

        def pump():
            while seq and ring.can_load(2):
                w_, k_ = seq.pop(0)
                gid = ring.load(w1c[k_] if w_ == 1 else w2c[k_], 2)
                (g1 if w_ == 1 else g2)[k_] = gid
        pump()
        for fc in range(32):
            k, f = divmod(fc, 4)
            ps1 = tpool[(2 * fc) % 4]
            ps2 = tpool[(2 * fc + 1) % 4]
            w = [(sv, sv1 + fc - 1)] if fc >= 2 else [(sv, sv1)]
            gg1, gg2 = g1[k], g2[k]
            for ic in range(NCH):
                p.emit("tensor", lambda t, ic=ic, f=f, gg1=gg1, sl=sl, ps1=ps1: t.matmul(
                    ps1[:, :], ring.view(gg1, ic * 512 + f * 128, 128),
                    p.lnx8[ic][:, sl], start=(ic == 0), stop=(ic == NCH - 1)),
                    waits=([(ring.ld, ring.wv(gg1))] + w) if ic == 0 else [])
            for ic in range(NCH):
                p.emit("tensor", lambda t, ic=ic, f=f, gg2=gg2, sl=sl, ps2=ps2: t.matmul(
                    ps2[:, :], ring.view(gg2, ic * 512 + f * 128, 128),
                    p.lnx8[ic][:, sl], start=(ic == 0), stop=(ic == NCH - 1)),
                    waits=[(ring.ld, ring.wv(gg2))] if ic == 0 else [],
                    inc=sp if ic == NCH - 1 else None)
            if dbg == "proj":
                if fc < 8:
                    p.emit("vector", lambda v, fc=fc, sl=sl, ps1=ps1: v.tensor_copy(
                        p.x8[fc][:, sl], ps1[:, :]), waits=[(sp, sp.n)], inc=sv)
                else:
                    p.emit("vector", lambda v: v.engine_nop(), waits=[(sp, sp.n)], inc=sv)
                ring.mark_free(g1[k], sp, sp.n)
                ring.mark_free(g2[k], sp, sp.n)
                if f == 3:
                    ensure(k + 2)
                continue
            wsc = [(sv, sv1 + fc)] if fc >= 1 else [(sv, sv1)]
            p.emit("scalar", lambda e, fc=fc, ps1=ps1: e.activation(
                p.scr1[:, :], ps1[:, :], AF.Silu, scale=s1[:, fc:fc + 1]),
                waits=[(sp, sp.n)] + wsc, inc=sa)
            if dbg == "s1a":
                if fc < 8:
                    p.emit("vector", lambda v, fc=fc, sl=sl: v.tensor_copy(
                        p.x8[fc][:, sl], p.scr1[:, :]), waits=[(sa, sa.n)], inc=sv)
                else:
                    p.emit("vector", lambda v: v.engine_nop(),
                           waits=[(sa, sa.n)], inc=sv)
            elif dbg == "s1c":
                if fc < 8:
                    p.emit("vector", lambda v, fc=fc, sl=sl, ps2=ps2: v.tensor_scalar_mul(
                        p.x8[fc][:, sl], ps2[:, :], s2[:, fc:fc + 1]),
                        waits=[(sa, sa.n)], inc=sv)
                else:
                    p.emit("vector", lambda v: v.engine_nop(),
                           waits=[(sa, sa.n)], inc=sv)
            else:
                p.emit("vector", lambda v, fc=fc, ps2=ps2: v.scalar_tensor_tensor(
                    A[fc][:, :], ps2[:, :], s2[:, fc:fc + 1], p.scr1[:, :],
                    ALU.mult, ALU.mult), waits=[(sa, sa.n)], inc=sv)
            if f == 3:
                ring.mark_free(g1[k], sp, sp.n)
                ring.mark_free(g2[k], sp, sp.n)
                pump()
        if dbg in ("proj", "s1a", "s1c"):
            sdbg = p.sem("dbg")
            p.wait("tensor", sv)
            p.barrier()
            break
        if dbg == "s1":
            sdbg = p.sem("dbg")
            for c in range(NCH):
                p.emit("vector", lambda v, c=c, sl=sl: v.tensor_copy(
                    p.x8[c][:, sl], A[c][:, :]), waits=[(sv, sv.n)], inc=sdbg)
            p.wait("tensor", sdbg)
            continue
        sa2 = sa.n
        wA = [(sv, sv.n)]
        for occ in range(32):
            g = ring.load(h1c[occ], 2)
            ps = tpool[occ % 4]
            wps = [(sa, sa2 + occ - 3)] if occ >= 4 else []
            for fc in range(32):
                p.emit("tensor", lambda t, fc=fc, g=g, ps=ps: t.matmul(
                    ps[:, :], ring.view(g, fc * 128, 128), A[fc][:, :],
                    start=(fc == 0), stop=(fc == 31)),
                    waits=([(ring.ld, ring.wv(g))] + wA + wps) if fc == 0 else [],
                    inc=sp if fc == 31 else None)
            p.emit("scalar", lambda e, occ=occ, ps=ps: e.activation(
                B[occ][:, :], ps[:, :], AF.Gelu, scale=sh1[:, occ:occ + 1]),
                waits=[(sp, sp.n)], inc=sa)
            ring.mark_free(g, sp, sp.n)
        if dbg == "s2":
            sdbg = p.sem("dbg")
            for c in range(NCH):
                p.emit("vector", lambda v, c=c, sl=sl: v.tensor_copy(
                    p.x8[c][:, sl], B[c][:, :]), waits=[(sa, sa.n)], inc=sdbg)
            p.wait("tensor", sdbg)
            continue
        sa3 = sa.n
        wB = [(sa, sa.n)]
        for occ in range(32):
            g = ring.load(h2c[occ], 2)
            ps = tpool[occ % 4]
            wps = [(sa, sa3 + occ - 3)] if occ >= 4 else []
            for fc in range(32):
                p.emit("tensor", lambda t, fc=fc, g=g, ps=ps: t.matmul(
                    ps[:, :], ring.view(g, fc * 128, 128), B[fc][:, :],
                    start=(fc == 0), stop=(fc == 31)),
                    waits=([(ring.ld, ring.wv(g))] + wB + wps) if fc == 0 else [],
                    inc=sp if fc == 31 else None)
            p.emit("scalar", lambda e, occ=occ, ps=ps: e.activation(
                A[occ][:, :], ps[:, :], AF.Gelu, scale=sh2[:, occ:occ + 1]),
                waits=[(sp, sp.n)], inc=sa)
            ring.mark_free(g, sp, sp.n)
        sv4 = sv.n
        wC = [(sa, sa.n)]
        for occ in range(8):
            g = ring.load(w3c[occ], 2)
            ps = tpool[occ % 4]
            wps = [(sv, sv4 + occ - 3)] if occ >= 4 else []
            for fc in range(32):
                p.emit("tensor", lambda t, fc=fc, g=g, ps=ps: t.matmul(
                    ps[:, :], ring.view(g, fc * 128, 128), A[fc][:, :],
                    start=(fc == 0), stop=(fc == 31)),
                    waits=([(ring.ld, ring.wv(g))] + wC + wps) if fc == 0 else [],
                    inc=sp if fc == 31 else None)
            p.emit("vector", lambda v, occ=occ, sl=sl, ps=ps: v.scalar_tensor_tensor(
                p.x8[occ][:, sl], ps[:, :], s3[:, occ:occ + 1], p.x8[occ][:, sl],
                ALU.mult, ALU.add), waits=[(sp, sp.n)], inc=sv)
            ring.mark_free(g, sp, sp.n)
        p.wait("tensor", sv)
    p.barrier()





def emit_moe(p, W, kind, dbg=None):
    nc = p.nc
    emit_ln(p, f"{kind}_ln")
    sp = p.sem(f"{kind}_p")
    sa = p.sem(f"{kind}_a")
    sv = p.sem(f"{kind}_v")
    ring = WRing(p, kind)

    p.pool_reset()
    gwTb = load_small(p, W["gwC"], 128, 64, BF16)
    if kind == "l1":
        rmsc = load_small(p, W["rms"], 128, 8)
        b1c = load_small(p, W["b1c"], 128, 256)
        b2c = load_small(p, W["b2c"], 128, 64)
    else:
        brr = load_small(p, W["br"], 1, 8)
    WT = p.pallocB(8, T)
    wtok = p.pallocF(128, 8)
    wtokb = p.pallocB(128, 128)
    probs = p.pallocF(128, 8)
    colA = p.pallocF(128, 4)
    brbc = p.pallocF(128, 8)
    act = AF.Silu if kind == "l1" else AF.Gelu

    wait_small(p, ENGINES)
    gcast = sv.n

    if kind == "l1":
        for tt in range(NTT):
            sl = slice(TT * tt, TT * (tt + 1))
            psQ, psR = p.pb[0], p.pb[1]
            sa_q = sa.n
            sp_q = sp.n
            for c in range(NCH):
                buf = p.sqt[c % 2]
                w = [(sp, sp_q + c - 1 if c >= 2 else sp_q)]
                p.emit("scalar", lambda x_, c=c, sl=sl, buf=buf: x_.activation(
                    buf[:, :], p.lnx8[c][:, sl], AF.Square), waits=w, inc=sa)
                p.emit("tensor", lambda t, c=c, buf=buf, psQ=psQ: t.matmul(
                    psQ[0:1, :], p.ones_col[:, :], buf[:, :],
                    start=(c == 0), stop=(c == NCH - 1)),
                    waits=[(sa, sa_q + c + 1)], inc=sp)
            p.emit("scalar", lambda x_, psQ=psQ: x_.mul(
                p.rowB[0:1, :], psQ[0:1, :], 1.0 / D), waits=[(sp, sp.n)], inc=sa)
            p.emit("scalar", lambda x_: x_.activation(
                p.rowB[0:1, :], p.rowB[0:1, :], AF.Ln, bias=1e-6), inc=sa)
            p.emit("scalar", lambda x_: x_.activation(
                p.rowB[0:1, :], p.rowB[0:1, :], AF.Exp, scale=-0.5), inc=sa)
            p.emit("vector", lambda v: v.tensor_copy(
                p.rowBb[0:1, :], p.rowB[0:1, :]), waits=[(sa, sa.n)], inc=sv)
            p.emit("tensor", lambda t, psR=psR: t.matmul(
                psR[:, :], p.ones_row[:, :], p.rowBb[0:1, :], start=True, stop=True),
                waits=[(sv, sv.n)], inc=sp)
            bc = sp.n
            for c in range(NCH):
                p.emit("vector", lambda v, c=c, sl=sl, psR=psR: v.tensor_mul(
                    p.scr0[:, :], p.lnx8[c][:, sl], psR[:, :]),
                    waits=[(sp, bc)], inc=sv)
                p.emit("vector", lambda v, c=c, sl=sl: v.tensor_scalar_mul(
                    p.lnx8[c][:, sl], p.scr0[:, :], rmsc[:, c:c + 1]), inc=sv)
            if tt < NTT - 1:
                p.wait("tensor", sv)
                p.wait("scalar", sv)
        p.barrier()

    if kind == "l2":
        p.emit("vector", lambda v: v.tensor_copy(
            p.rowBb[0:1, 0:8], brr[0:1, :]), inc=sv)
        p.emit("tensor", lambda t: t.matmul(
            p.pb[5][:, 0:8], p.ones_row[:, :], p.rowBb[0:1, 0:8],
            start=True, stop=True), waits=[(sv, sv.n)], inc=sp)
        p.emit("vector", lambda v: v.tensor_copy(
            brbc[:, :], p.pb[5][:, 0:8]), waits=[(sp, sp.n)], inc=sv)

    for q in range(T // 128):
        ql = slice(128 * q, 128 * (q + 1))
        psG = p.pb[4]
        wg = [(sv, sv.n)]
        for c in range(NCH):
            p.emit("tensor", lambda t, c=c, ql=ql, psG=psG: t.matmul(
                psG[0:128, 0:8], p.lnx8[c][:, ql], gwTb[:, 8 * c:8 * c + 8],
                start=(c == 0), stop=(c == NCH - 1)),
                waits=wg if c == 0 else [], inc=sp if c == NCH - 1 else None)
        if kind == "l1":
            p.emit("vector", lambda v, psG=psG: v.reduce_max(
                colA[:, 0:1], psG[0:128, 0:8], AX.X, negate=True),
                waits=[(sp, sp.n)], inc=sv)
            p.emit("scalar", lambda x_, psG=psG: x_.activation(
                probs[:, :], psG[0:128, 0:8], AF.Exp, bias=colA[:, 0:1]),
                waits=[(sv, sv.n)], inc=sa)
            p.emit("vector", lambda v: v.reduce_sum(
                colA[:, 1:2], probs[:, :], AX.X), waits=[(sa, sa.n)], inc=sv)
            p.emit("vector", lambda v: v.reciprocal(
                colA[:, 2:3], colA[:, 1:2]), waits=[(sv, sv.n)], inc=sv)
            p.emit("vector", lambda v: v.tensor_scalar_mul(
                probs[:, :], probs[:, :], colA[:, 2:3]), waits=[(sv, sv.n)], inc=sv)
        else:
            p.emit("vector", lambda v, psG=psG: v.tensor_add(
                probs[:, :], psG[0:128, 0:8], brbc[:, :]),
                waits=[(sp, sp.n), (sv, sv.n)], inc=sv)
            p.emit("scalar", lambda x_: x_.activation(
                probs[:, :], probs[:, :], AF.Sigmoid), waits=[(sv, sv.n)], inc=sa)
            p.wait("vector", sa)
        p.emit("vector", lambda v: v.reduce_max(
            colA[:, 3:4], probs[:, :], AX.X), waits=[(sv, sv.n)], inc=sv)
        p.emit("vector", lambda v: v.tensor_scalar(
            wtok, probs, colA[:, 3:4], colA[:, 3:4],
            ALU.is_equal, ALU.mult), waits=[(sv, sv.n)], inc=sv)
        p.emit("vector", lambda v: v.tensor_copy(wtokb[:, 0:8], wtok),
               waits=[(sv, sv.n)], inc=sv)
        p.emit("tensor", lambda t, q=q: t.transpose(
            p.pt[0:128, 0:128], wtokb, p.ident[:, :]),
            waits=[(sv, sv.n)], inc=sp)
        p.emit("vector", lambda v, q=q, ql=ql: v.tensor_copy(
            WT[0:8, ql], p.pt[0:8, 0:128]), waits=[(sp, sp.n)], inc=sv)
        if dbg == "gate":
            p.emit("vector", lambda v, q=q: v.tensor_copy(
                p.dbgP[:, 8 * q:8 * q + 8], probs), inc=sv)
            p.emit("vector", lambda v, q=q: v.tensor_copy(
                p.dbgW[:, 8 * q:8 * q + 8], wtok), inc=sv)
    p.barrier()
    p.moe_WT = WT
    if dbg == "gate":
        return

    HT = 1024
    H = [(p.bigA[:, i * HT:(i + 1) * HT] if i < 16 else
          p.bigB[:, (i - 16) * HT:(i - 15) * HT]) for i in range(32)]
    bank_free = {}
    for e in range(8):
        for tb in range(T // HT):
            base = tb * HT
            for fc in range(32):
                k = fc // 4
                if fc % 4 == 0:
                    g = ring.load(W["w1B"][e][k], 2)
                for th in range(2):
                    sl = slice(base + 512 * th, base + 512 * th + 512)
                    bk = (2 * fc + th) % 4
                    ps = p.pb[bk]
                    wps = [bank_free[bk]] if bk in bank_free else [(sv, sv.n)]
                    for ic in range(NCH):
                        p.emit("tensor", lambda t, ic=ic, fc=fc, g=g, sl=sl, ps=ps: t.matmul(
                            ps[:, :], ring.view(g, ic * 512 + (fc % 4) * 128, 128),
                            p.lnx8[ic][:, sl], start=(ic == 0), stop=(ic == NCH - 1)),
                            waits=([(ring.ld, ring.wv(g))] + wps) if ic == 0 else [],
                            inc=sp if ic == NCH - 1 else None)
                    if kind == "l1":
                        p.emit("scalar", lambda x_, fc=fc, th=th, e=e, ps=ps: x_.activation(
                            H[fc][:, 512 * th:512 * th + 512], ps[:, :], act,
                            bias=b1c[:, 32 * e + fc:32 * e + fc + 1]),
                            waits=[(sp, sp.n)], inc=sa)
                    else:
                        p.emit("scalar", lambda x_, fc=fc, th=th, ps=ps: x_.activation(
                            H[fc][:, 512 * th:512 * th + 512], ps[:, :], act),
                            waits=[(sp, sp.n)], inc=sa)
                    bank_free[bk] = (sa, sa.n)
                if fc % 4 == 3:
                    ring.mark_free(g, sp, sp.n)
            wH = [(sa, sa.n)]
            for th in range(2):
                sl = slice(base + 512 * th, base + 512 * th + 512)
                w4 = [bank_free[4]] if 4 in bank_free else []
                p.emit("tensor", lambda t, e=e, sl=sl: t.matmul(
                    p.pb[4][:, :], p.selb[0:8, 128 * e:128 * e + 128], WT[0:8, sl],
                    start=True, stop=True),
                    waits=wH + w4, inc=sp)
                p.emit("vector", lambda v: v.tensor_copy(
                    p.scr0[:, :], p.pb[4][:, :]), waits=[(sp, sp.n)], inc=sv)
                bank_free[4] = (sv, sv.n)
                svw = sv.n
                for oc in range(8):
                    g = ring.load(W["w2B"][e][oc], 2)
                    bk = oc % 4
                    ps = p.pb[bk]
                    wps = [bank_free[bk]] if bk in bank_free else []
                    for fc in range(32):
                        p.emit("tensor", lambda t, fc=fc, g=g, th=th, ps=ps: t.matmul(
                            ps[:, :], ring.view(g, fc * 128, 128),
                            H[fc][:, 512 * th:512 * th + 512],
                            start=(fc == 0), stop=(fc == 31)),
                            waits=([(ring.ld, ring.wv(g))] + wps) if fc == 0 else [],
                            inc=sp if fc == 31 else None)
                    if kind == "l1":
                        p.emit("vector", lambda v, oc=oc, e=e, ps=ps: v.scalar_tensor_tensor(
                            p.scr1[:, :], ps[:, :], b2c[:, 8 * e + oc:8 * e + oc + 1],
                            p.scr0[:, :], ALU.add, ALU.mult),
                            waits=[(sp, sp.n), (sv, svw)], inc=sv)
                    else:
                        p.emit("vector", lambda v, oc=oc, ps=ps: v.tensor_mul(
                            p.scr1[:, :], ps[:, :], p.scr0[:, :]),
                            waits=[(sp, sp.n), (sv, svw)], inc=sv)
                    p.emit("vector", lambda v, oc=oc, sl=sl: v.tensor_add(
                        p.x8[oc][:, sl], p.x8[oc][:, sl], p.scr1[:, :]), inc=sv)
                    bank_free[bk] = (sv, sv.n)
                    ring.mark_free(g, sp, sp.n)
            p.wait("tensor", sv)
    p.barrier()


def prep_moe_l1(inp):
    W1 = np.asarray(inp["L1_ffn_W1"], np.float32)
    W2 = np.asarray(inp["L1_ffn_W2"], np.float32)
    gw = _t(inp["L1_ffn_gate_w"])
    return {
        "gwC": np.ascontiguousarray(gw.reshape(8, 128, 8).transpose(1, 0, 2).reshape(128, 64)),
        "rms": _colmajor(inp["L1_ffn_rms_w"], 8),
        "b1c": np.concatenate([_colmajor(inp["L1_ffn_b1"][e], 32) for e in range(8)], 1),
        "b2c": np.concatenate([_colmajor(inp["L1_ffn_b2"][e], 8) for e in range(8)], 1),
        "w1B": np.stack([host_blocks(W1[e], 512) for e in range(8)]),
        "w2B": np.stack([host_blocks(W2[e], 128) for e in range(8)]),
    }


def prep_moe_l2(inp):
    W1 = np.asarray(inp["L2_ffn_W1"], np.float32)
    W2 = np.asarray(inp["L2_ffn_W2"], np.float32)
    gw = np.ascontiguousarray(np.asarray(inp["L2_ffn_Wr"], np.float32))
    return {
        "gwC": np.ascontiguousarray(gw.reshape(8, 128, 8).transpose(1, 0, 2).reshape(128, 64)),
        "br": np.asarray(inp["L2_ffn_br"], np.float32).reshape(1, 8),
        "w1B": np.stack([host_blocks(W1[e], 512) for e in range(8)]),
        "w2B": np.stack([host_blocks(W2[e], 128) for e in range(8)]),
    }


MOE_L1_SHAPES = {
    "gwC": (128, 64), "rms": (128, 8), "b1c": (128, 256), "b2c": (128, 64),
    "w1B": (8, 8, 128, 4096), "w2B": (8, 8, 128, 4096),
}
MOE_L2_SHAPES = {
    "gwC": (128, 64), "br": (1, 8),
    "w1B": (8, 8, 128, 4096), "w2B": (8, 8, 128, 4096),
}


def emit_l1_ffn(p, W):
    emit_moe(p, W, "l1")


def emit_l2_ffn(p, W):
    emit_moe(p, W, "l2")



def emit_l3_ffn(p, W):
    nc = p.nc
    emit_ln(p, "l3f_ln")
    sp = p.sem("l3f_p")
    sa = p.sem("l3f_a")
    sv = p.sem("l3f_v")
    ring = WRing(p, "l3f")
    p.pool_reset()
    lng = load_small(p, W["lng"], 128, 8)
    lnb = load_small(p, W["lnb"], 128, 8)
    gam = load_small(p, W["gam"], 128, 8)
    brrowb = load_small(p, W["rb"], 1, 512, BF16)
    hwT = p.pallocB(128, 512)
    hwb = p.pallocB(128, 512)
    vnorm = p.pallocF(128, 8)
    top8c = p.pallocF(128, 16)
    scrE = p.scrE
    wait_small(p)

    A = [p.bigA[:, i * TT:(i + 1) * TT] for i in range(32)]
    w1c, w2c = W["w1B"], W["w2B"]

    for tt in range(NTT):
        sl = slice(TT * tt, TT * (tt + 1))
        sa1 = sa.n
        for fc in range(32):
            if fc % 2 == 0:
                g = ring.load(w1c[fc // 2], 1)
            ps = p.pb[fc % 4]
            wps = [(sa, sa1 + fc - 3)] if fc >= 4 else [(sv, sv.n)]
            for ic in range(NCH):
                p.emit("tensor", lambda t, ic=ic, fc=fc, g=g, sl=sl, ps=ps: t.matmul(
                    ps[:, :], ring.view(g, ic * 256 + (fc % 2) * 128, 128),
                    p.lnx8[ic][:, sl], start=(ic == 0), stop=(ic == NCH - 1)),
                    waits=([(ring.ld, ring.wv(g))] + wps) if ic == 0 else [],
                    inc=sp if ic == NCH - 1 else None)
            p.emit("scalar", lambda x_, fc=fc, ps=ps: x_.activation(
                A[fc][:, :], ps[:, :], AF.Gelu), waits=[(sp, sp.n)], inc=sa)
            if fc % 2 == 1:
                ring.mark_free(g, sp, sp.n)
        sv4 = sv.n
        wC = [(sa, sa.n)]
        for occ in range(8):
            g = ring.load(w2c[occ], 2)
            ps = p.pb[occ % 4]
            wps = [(sv, sv4 + occ - 3)] if occ >= 4 else []
            for fc in range(32):
                p.emit("tensor", lambda t, fc=fc, g=g, ps=ps: t.matmul(
                    ps[:, :], ring.view(g, fc * 128, 128), A[fc][:, :],
                    start=(fc == 0), stop=(fc == 31)),
                    waits=([(ring.ld, ring.wv(g))] + wC + wps) if fc == 0 else [],
                    inc=sp if fc == 31 else None)
            p.emit("vector", lambda v, occ=occ, sl=sl, ps=ps: v.tensor_add(
                p.x8[occ][:, sl], p.x8[occ][:, sl], ps[:, :]),
                waits=[(sp, sp.n)], inc=sv)
            ring.mark_free(g, sp, sp.n)
        p.wait("tensor", sv)
    p.barrier()

    rln8 = [p.bigB[:, 2048 * c:2048 * (c + 1)] for c in range(8)]
    emit_ln(p, "l3f_iln", src=p.lnx8, src_bf=p.lnx8, dst=rln8,
            gain_col=[lng[:, c:c + 1] for c in range(8)],
            bias_col=[lnb[:, c:c + 1] for c in range(8)])

    sdd = p.sem("l3f_dd")
    rwt = p.bigA[:, 0:4096]
    ut = p.bigA[:, 4096:8192]
    vch = [p.bigA[:, 8192 + 1024 * i:8192 + 1024 * (i + 1)] for i in range(4)]
    p.emit("gpsimd", lambda g: g.dma_start(out=rwt, in_=W["rwC"]), inc=sdd, k=16)
    p.emit("gpsimd", lambda g: g.dma_start(out=ut, in_=W["uC"]), inc=sdd, k=16)
    for i in range(4):
        p.emit("gpsimd", lambda g, i=i: g.dma_start(
            out=vch[i], in_=W["vB"][i]), inc=sdd, k=16)
    for e in ("tensor", "scalar", "vector"):
        p.wait(e, sdd)
    for ic in range(NCH):
        buf = p.sqt[ic % 2]
        w = [(sp, sp.n)] if ic >= 2 else []
        p.emit("scalar", lambda x_, ic=ic, buf=buf: x_.activation(
            buf[:, :], ut[:, 512 * ic:512 * (ic + 1)], AF.Square), waits=w, inc=sa)
        p.emit("tensor", lambda t, ic=ic, buf=buf: t.matmul(
            p.pb[4][0:1, :], p.ones_col[:, :], buf[:, :],
            start=(ic == 0), stop=(ic == NCH - 1)),
            waits=[(sa, sa.n)], inc=sp)
    p.emit("scalar", lambda x_: x_.activation(
        p.rowA[0:1, :], p.pb[4][0:1, :], AF.Sqrt), waits=[(sp, sp.n)], inc=sa)
    p.emit("vector", lambda v: v.tensor_scalar_max(
        p.rowA[0:1, :], p.rowA[0:1, :], 1e-6), waits=[(sa, sa.n)], inc=sv)
    p.emit("vector", lambda v: v.reciprocal(
        p.rowB[0:1, :], p.rowA[0:1, :]), waits=[(sv, sv.n)], inc=sv)
    p.emit("vector", lambda v: v.tensor_copy(
        p.rowBb[0:1, :], p.rowB[0:1, :]), waits=[(sv, sv.n)], inc=sv)
    p.emit("tensor", lambda t: t.matmul(
        p.pb[5][:, :], p.ones_row[:, :], p.rowBb[0:1, :], start=True, stop=True),
        waits=[(sv, sv.n)], inc=sp)
    p.emit("vector", lambda v: v.tensor_copy(
        p.scr0[:, :], p.pb[5][:, :]), waits=[(sp, sp.n)], inc=sv)
    for i in range(4):
        p.emit("scalar", lambda x_, i=i: x_.activation(
            p.scr1[:, :], vch[i][:, 0:512], AF.Square), waits=[(sv, sv.n)], inc=sa)
        p.emit("vector", lambda v, i=i: v.reduce_sum(
            vnorm[:, 0:1], p.scr1[:, :], AX.X), waits=[(sa, sa.n)], inc=sv)
        p.emit("scalar", lambda x_, i=i: x_.activation(
            p.scr1[:, :], vch[i][:, 512:1024], AF.Square), waits=[(sv, sv.n)], inc=sa)
        p.emit("vector", lambda v, i=i: v.reduce_sum(
            vnorm[:, 1:2], p.scr1[:, :], AX.X), waits=[(sa, sa.n)], inc=sv)
        p.emit("vector", lambda v, i=i: v.tensor_add(
            vnorm[:, 0:1], vnorm[:, 0:1], vnorm[:, 1:2]), waits=[(sv, sv.n)], inc=sv)
        p.emit("scalar", lambda x_, i=i: x_.activation(
            vnorm[:, 1:2], vnorm[:, 0:1], AF.Sqrt), waits=[(sv, sv.n)], inc=sa)
        p.emit("vector", lambda v, i=i: v.tensor_scalar_max(
            vnorm[:, 1:2], vnorm[:, 1:2], 1e-6), waits=[(sa, sa.n)], inc=sv)
        p.emit("vector", lambda v, i=i: v.reciprocal(
            vnorm[:, 2:3], vnorm[:, 1:2]), waits=[(sv, sv.n)], inc=sv)
        p.emit("vector", lambda v, i=i: v.tensor_scalar_mul(
            vch[i][:, :], vch[i][:, :], vnorm[:, 2:3]), waits=[(sv, sv.n)], inc=sv)
    p.barrier()

    for q in range(T // 128):
        ql = slice(128 * q, 128 * (q + 1))
        psR = p.pb[4]
        psH = p.pb[3]
        alpha = p.scr1
        for ic in range(NCH):
            p.emit("tensor", lambda t, ic=ic, ql=ql, psR=psR: t.matmul(
                psR[:, :], rln8[ic][:, ql], rwt[:, 512 * ic:512 * (ic + 1)],
                start=(ic == 0), stop=False),
                waits=[(sv, sv.n)] if ic == 0 else [],
                inc=None)
        p.emit("tensor", lambda t, psR=psR: t.matmul(
            psR[:, :], p.ones_row[0:1, :], brrowb[0:1, :], start=False, stop=True),
            inc=sp)
        p.emit("vector", lambda v, psR=psR: v.tensor_scalar_min(
            alpha[:, :], psR[:, :], 10.0), waits=[(sp, sp.n)], inc=sv)
        p.emit("vector", lambda v: v.tensor_scalar_max(
            alpha[:, :], alpha[:, :], -10.0), waits=[(sv, sv.n)], inc=sv)
        p.emit("scalar", lambda x_: x_.activation(
            scrE[:, :], alpha[:, :], AF.Abs),
            waits=[(sv, sv.n)], inc=sa)
        p.emit("scalar", lambda x_: x_.activation(
            scrE[:, :], scrE[:, :], AF.Exp, scale=-1.0),
            waits=[(sa, sa.n)], inc=sa)
        p.emit("scalar", lambda x_: x_.activation(
            scrE[:, :], scrE[:, :], AF.Ln, bias=1.0), inc=sa)
        p.emit("vector", lambda v: v.scalar_tensor_tensor(
            alpha[:, :], alpha[:, :], 0.0, scrE[:, :], ALU.max, ALU.add),
            waits=[(sa, sa.n)], inc=sv)
        p.emit("vector", lambda v: v.max(
            top8c[:, 0:8], alpha[:, :]), waits=[(sv, sv.n)], inc=sv)
        p.emit("vector", lambda v: v.reduce_sum(
            top8c[:, 8:9], top8c[:, 0:8], AX.X), waits=[(sv, sv.n)], inc=sv)
        p.emit("scalar", lambda x_: x_.activation(
            top8c[:, 9:10], top8c[:, 8:9], AF.Tanh), waits=[(sv, sv.n)], inc=sa)
        p.emit("vector", lambda v: v.tensor_scalar_add(
            top8c[:, 10:11], top8c[:, 8:9], 1e-6), waits=[(sa, sa.n)], inc=sv)
        p.emit("vector", lambda v: v.reciprocal(
            top8c[:, 11:12], top8c[:, 10:11]), waits=[(sv, sv.n)], inc=sv)
        p.emit("vector", lambda v: v.tensor_mul(
            top8c[:, 12:13], top8c[:, 9:10], top8c[:, 11:12]),
            waits=[(sv, sv.n)], inc=sv)
        p.emit("vector", lambda v: v.tensor_scalar(
            scrE[:, :], alpha[:, :], top8c[:, 7:8], top8c[:, 12:13],
            ALU.is_ge, ALU.mult), waits=[(sv, sv.n)], inc=sv)
        p.emit("vector", lambda v: v.tensor_mul(
            scrE[:, :], scrE[:, :], alpha[:, :]), waits=[(sv, sv.n)], inc=sv)
        p.emit("vector", lambda v: v.tensor_mul(
            scrE[:, :], scrE[:, :], p.scr0[:, :]), waits=[(sv, sv.n)], inc=sv)
        for ic in range(NCH):
            p.emit("tensor", lambda t, ic=ic, ql=ql, psH=psH: t.matmul(
                psH[:, :], p.lnx8[ic][:, ql], ut[:, 512 * ic:512 * (ic + 1)],
                start=(ic == 0), stop=(ic == NCH - 1)),
                waits=[(sv, sv.n)] if ic == 0 else [],
                inc=sp if ic == NCH - 1 else None)
        p.emit("vector", lambda v, psH=psH: v.tensor_mul(
            hwb[:, :], psH[:, :], scrE[:, :]), waits=[(sp, sp.n), (sv, sv.n)],
            inc=sv)
        for j in range(4):
            p.emit("tensor", lambda t, j=j: t.transpose(
                p.pt[0:128, 128 * j:128 * (j + 1)], hwb[:, 128 * j:128 * (j + 1)],
                p.ident[:, :]), waits=[(sv, sv.n)] if j == 0 else [], inc=sp)
        p.emit("vector", lambda v: v.tensor_copy(
            hwT[:, :], p.pt[0:128, 0:512]), waits=[(sp, sp.n)], inc=sv)
        svq = sv.n
        for oc in range(8):
            psD = p.pb[oc % 3]
            wps = [(sv, svq + oc - 2)] if oc >= 3 else [(sv, svq)]
            for nbc in range(4):
                p.emit("tensor", lambda t, oc=oc, nbc=nbc, psD=psD: t.matmul(
                    psD[:, 0:128], vch[nbc][:, 128 * oc:128 * (oc + 1)],
                    hwT[:, 128 * nbc:128 * (nbc + 1)],
                    start=(nbc == 0), stop=(nbc == 3)),
                    waits=wps if nbc == 0 else [], inc=sp if nbc == 3 else None)
            p.emit("vector", lambda v, oc=oc, ql=ql, psD=psD: v.scalar_tensor_tensor(
                p.x8[oc][:, ql], psD[:, 0:128], gam[:, oc:oc + 1], p.x8[oc][:, ql],
                ALU.mult, ALU.add), waits=[(sp, sp.n)], inc=sv)
    p.barrier()


def prep_l3_ffn(inp):
    W1T = _t(inp["L3_ffn_W1"])
    W2T = _t(inp["L3_ffn_W2"])
    rwT = _t(inp["L3_ffn_router_W"])
    uT = _t(inp["L3_ffn_raw_U"])
    V = np.asarray(inp["L3_ffn_raw_V"], np.float32)
    def chunkcat(M):
        return np.ascontiguousarray(
            M.reshape(8, 128, 512).transpose(1, 0, 2).reshape(128, 4096))
    return {
        "w1B": host_blocks(W1T, 256),
        "w2B": host_blocks(W2T, 128),
        "rwC": chunkcat(rwT),
        "uC": chunkcat(uT),
        "vB": np.ascontiguousarray(V.reshape(4, 128, 1024)),
        "lng": _colmajor(inp["L3_ffn_ln_g"], 8),
        "lnb": _colmajor(inp["L3_ffn_ln_b"], 8),
        "gam": _colmajor(np.asarray(inp["L3_ffn_gamma"]).reshape(1024), 8),
        "rb": np.asarray(inp["L3_ffn_router_b"], np.float32).reshape(1, 512),
    }


L3FFN_SHAPES = {
    "w1B": (16, 128, 2048), "w2B": (8, 128, 4096), "rwC": (128, 4096),
    "uC": (128, 4096), "vB": (4, 128, 1024), "lng": (128, 8), "lnb": (128, 8),
    "gam": (128, 8), "rb": (1, 512),
}



def emit_l0_attn(p, W):
    nc = p.nc
    emit_ln(p, "l0a_ln")
    sp = p.sem("l0a_p")
    sa = p.sem("l0a_a")
    sv = p.sem("l0a_v")
    sg = p.sem("l0a_g")
    sy = p.sem("l0a_y")
    cc = p.sem("l0a_cc")
    ring = WRing(p, "l0a")
    p.pool_reset()
    binc = load_small(p, W["binc"], 128, 16)
    convw = load_small(p, W["convw"], 128, 64)
    convb = load_small(p, W["convb"], 128, 16)
    bout = load_small(p, W["bout"], 128, 8)
    bxdt = load_small(p, W["bxdt"], 64, 2)
    alog = load_small(p, W["alog"], 64, 1)
    selc = load_small(p, W["l0sel"], 64, 4)
    hval = load_small(p, W["hval"], 128, 1)
    xh = load_small(p, W["xh"], 128, 24, BF16)
    wxdt = p.pallocB(128, 2048)
    lnxh = p.pallocB(128, 24)
    scc = p.pallocF(64, 16)
    p.emit("gpsimd", lambda g: g.dma_start(out=wxdt, in_=W["wxdtC"]),
           inc=p.sem("smldg"), k=16)
    wait_small(p)

    bigBf = p.bigB[:, :].bitcast(F32)
    B0 = bigBf[0:64, 0:2048]
    B1 = bigBf[0:64, 2048:4096]
    B2 = bigBf[0:64, 4096:6144]
    B3 = bigBf[0:64, 6144:8192]
    hc = p.bigA[:, 0:2052]
    hcv = p.bigA[:, 2052:4100]
    zb = p.bigA[0:64, 4100:6148]
    M = p.bigA[0:64, 6148:7172]
    winc = W["winB"]

    psS, psQ = p.pb[0], p.pb[1]
    for c in range(NCH):
        p.emit("tensor", lambda t, c=c, psS=psS: t.matmul(
            psS[0:1, 0:3], p.ones_col[:, :], xh[:, 3 * c:3 * c + 3],
            start=(c == 0), stop=(c == NCH - 1)), inc=sp if c == NCH - 1 else None)
    for c in range(NCH):
        p.emit("scalar", lambda x_, c=c: x_.activation(
            p.sqt[0][:, 3 * c:3 * c + 3], xh[:, 3 * c:3 * c + 3], AF.Square),
            inc=sa)
    sqdone = sa.n
    for c in range(NCH):
        p.emit("tensor", lambda t, c=c, psQ=psQ: t.matmul(
            psQ[0:1, 0:3], p.ones_col[:, :], p.sqt[0][:, 3 * c:3 * c + 3],
            start=(c == 0), stop=(c == NCH - 1)),
            waits=[(sa, sqdone)] if c == 0 else [], inc=sp if c == NCH - 1 else None)
    p.emit("scalar", lambda x_, psS=psS: x_.mul(
        p.rowA[0:1, 0:3], psS[0:1, 0:3], 1.0 / D), waits=[(sp, sp.n)], inc=sa)
    p.emit("scalar", lambda x_, psQ=psQ: x_.mul(
        p.rowB[0:1, 0:3], psQ[0:1, 0:3], 1.0 / D), waits=[(sa, sa.n)], inc=sa)
    p.emit("vector", lambda v: v.tensor_mul(
        p.rowA[0:1, 4:7], p.rowA[0:1, 0:3], p.rowA[0:1, 0:3]),
        waits=[(sa, sa.n)], inc=sv)
    p.emit("vector", lambda v: v.tensor_sub(
        p.rowB[0:1, 0:3], p.rowB[0:1, 0:3], p.rowA[0:1, 4:7]),
        waits=[(sv, sv.n)], inc=sv)
    p.emit("scalar", lambda x_: x_.activation(
        p.rowB[0:1, 0:3], p.rowB[0:1, 0:3], AF.Ln, bias=1e-5),
        waits=[(sv, sv.n)], inc=sa)
    p.emit("scalar", lambda x_: x_.activation(
        p.rowB[0:1, 0:3], p.rowB[0:1, 0:3], AF.Exp, scale=-0.5),
        waits=[(sa, sa.n)], inc=sa)
    p.emit("vector", lambda v: v.tensor_copy(
        p.rowAb[0:1, 0:3], p.rowA[0:1, 0:3]), waits=[(sa, sa.n)], inc=sv)
    p.emit("vector", lambda v: v.tensor_copy(
        p.rowBb[0:1, 0:3], p.rowB[0:1, 0:3]), waits=[(sv, sv.n)], inc=sv)
    p.emit("tensor", lambda t: t.matmul(
        p.pb[2][:, 0:3], p.ones_row[:, :], p.rowAb[0:1, 0:3],
        start=True, stop=True), waits=[(sv, sv.n)], inc=sp)
    p.emit("tensor", lambda t: t.matmul(
        p.pb[3][:, 0:3], p.ones_row[:, :], p.rowBb[0:1, 0:3],
        start=True, stop=True), inc=sp)
    for c in range(NCH):
        p.emit("vector", lambda v, c=c: v.tensor_sub(
            p.scr0[:, 0:3], xh[:, 3 * c:3 * c + 3], p.pb[2][:, 0:3]),
            waits=[(sp, sp.n), (sv, sv.n)], inc=sv)
        p.emit("vector", lambda v, c=c: v.tensor_mul(
            lnxh[:, 3 * c:3 * c + 3], p.scr0[:, 0:3], p.pb[3][:, 0:3]),
            waits=[(sv, sv.n)], inc=sv)
    p.barrier()

    gD = None
    for half in range(2):
        gW = ring.load(W["woB"][half], 4)
        if gD is None:
            gD = ring.load(W["dTC"], 1)
        for ic in range(16):
            p.emit("tensor", lambda t, ic=ic, gW=gW, gD=gD: t.matmul(
                p.pb[4][0:64, :], ring.view(gD, ic * 64, 64),
                ring.view(gW, ic * 512, 512),
                start=(ic == 0), stop=(ic == 15)),
                waits=[(ring.ld, ring.wv(gW)), (ring.ld, ring.wv(gD))]
                      if ic == 0 else [],
                inc=sp if ic == 15 else None)
        p.emit("vector", lambda v, half=half: v.tensor_copy(
            M[:, 512 * half:512 * (half + 1)], p.pb[4][0:64, :]),
            waits=[(sp, sp.n)], inc=sv)
        ring.mark_free(gW, sp, sp.n)
    ring.mark_free(gD, sp, sp.n)
    p.wait("tensor", sv)

    hfree = {}
    for c in range(16):
        if c % 2 == 0:
            g = ring.load(winc[c // 2], 1)
        for tt in range(NTT):
            sl = slice(TT * tt, TT * (tt + 1))
            bk = tt % 2
            ps = p.pb[bk]
            wps = [hfree[bk]] if bk in hfree else []
            if tt == 0:
                wps = wps + [(sv, sv.n)]
            for ic in range(NCH):
                p.emit("tensor", lambda t, ic=ic, c=c, g=g, sl=sl, ps=ps: t.matmul(
                    ps[:, :], ring.view(g, ic * 256 + (c % 2) * 128, 128),
                    p.lnx8[ic][:, sl], start=(ic == 0), stop=(ic == NCH - 1)),
                    waits=([(ring.ld, ring.wv(g))] + wps) if ic == 0 else [],
                    inc=sp if ic == NCH - 1 else None)
            p.emit("scalar", lambda x_, c=c, tt=tt, ps=ps: x_.add(
                hc[:, 3 + TT * tt:3 + TT * (tt + 1)], ps[:, :],
                binc[:, c:c + 1]), waits=[(sp, sp.n)], inc=sa)
            hfree[bk] = (sa, sa.n)
        wh = [(sv, sv.n)]
        for ic in range(NCH):
            p.emit("tensor", lambda t, ic=ic, c=c, g=g: t.matmul(
                p.pb[2][:, 0:3], ring.view(g, ic * 256 + (c % 2) * 128, 128),
                lnxh[:, 3 * ic:3 * ic + 3], start=(ic == 0), stop=(ic == NCH - 1)),
                waits=wh if ic == 0 else [], inc=sp if ic == NCH - 1 else None)
        p.emit("vector", lambda v, c=c: v.tensor_scalar(
            p.scr1[:, 0:3], p.pb[2][:, 0:3], binc[:, c:c + 1], None, ALU.add),
            waits=[(sp, sp.n)], inc=sv)
        p.emit("vector", lambda v, c=c: v.tensor_scalar_mul(
            hc[:, 0:3], p.scr1[:, 0:3], hval[:, 0:1]), waits=[(sv, sv.n)], inc=sv)
        if c % 2 == 1:
            ring.mark_free(g, sp, sp.n)
        wA = [(sa, sa.n), (sp, sp.n)]
        for tt in range(NTT):
            o0 = 3 + TT * tt
            p.emit("vector", lambda v, c=c, o0=o0: v.tensor_scalar_mul(
                p.scr0[:, :], hc[:, o0 - 3:o0 - 3 + TT], convw[:, 4 * c:4 * c + 1]),
                waits=wA if tt == 0 else [], inc=sv)
            for k in range(1, 4):
                p.emit("vector", lambda v, c=c, o0=o0, k=k: v.scalar_tensor_tensor(
                    p.scr0[:, :], hc[:, o0 - 3 + k:o0 - 3 + k + TT],
                    convw[:, 4 * c + k:4 * c + k + 1], p.scr0[:, :],
                    ALU.mult, ALU.add), inc=sv)
            p.emit("vector", lambda v, c=c, o0=o0, tt=tt: v.tensor_scalar_add(
                hcv[:, TT * tt:TT * (tt + 1)], p.scr0[:, :], convb[:, c:c + 1]),
                inc=sv)
        for tt in range(NTT):
            sl = slice(TT * tt, TT * (tt + 1))
            p.emit("tensor", lambda t, c=c, tt=tt: t.matmul(
                p.pb[4][0:64, :], wxdt[:, 128 * c:128 * c + 64],
                hcv[:, TT * tt:TT * (tt + 1)], start=True, stop=True),
                waits=[(sv, sv.n)], inc=sp)
            p.emit("tensor", lambda t, c=c, tt=tt: t.matmul(
                p.pb[5][0:64, :], wxdt[:, 128 * c + 64:128 * c + 128],
                hcv[:, TT * tt:TT * (tt + 1)], start=True, stop=True), inc=sp)
            if c == 0:
                p.emit("vector", lambda v, sl=sl: v.tensor_copy(
                    B0[:, sl], p.pb[4][0:64, :]), waits=[(sp, sp.n)], inc=sv)
                p.emit("vector", lambda v, sl=sl: v.tensor_copy(
                    B1[:, sl], p.pb[5][0:64, :]), inc=sv)
            else:
                p.emit("vector", lambda v, sl=sl: v.tensor_add(
                    B0[:, sl], B0[:, sl], p.pb[4][0:64, :]), waits=[(sp, sp.n)], inc=sv)
                p.emit("vector", lambda v, sl=sl: v.tensor_add(
                    B1[:, sl], B1[:, sl], p.pb[5][0:64, :]), inc=sv)
    p.barrier()

    Acol = scc[:, 8:9]
    p.emit("scalar", lambda x_: x_.activation(
        scc[:, 9:10], alog[0:64, 0:1], AF.Exp), inc=sa)
    p.emit("vector", lambda v: v.tensor_scalar_mul(
        Acol, scc[:, 9:10], -1.0), waits=[(sa, sa.n)], inc=sv)
    p.emit("vector", lambda v: v.reciprocal(
        scc[:, 10:11], Acol), waits=[(sv, sv.n)], inc=sv)
    p.emit("vector", lambda v: v.tensor_scalar_add(
        B0[:, :], B0[:, :], bxdt[0:64, 0:1]), inc=sv)
    p.emit("vector", lambda v: v.tensor_scalar_add(
        B1[:, :], B1[:, :], bxdt[0:64, 1:2]), inc=sv)
    p.emit("scalar", lambda x_: x_.activation(
        B2[:, :], B1[:, :], AF.Abs), waits=[(sv, sv.n)], inc=sa)
    p.emit("scalar", lambda x_: x_.activation(
        B2[:, :], B2[:, :], AF.Exp, scale=-1.0), inc=sa)
    p.emit("scalar", lambda x_: x_.activation(
        B2[:, :], B2[:, :], AF.Ln, bias=1.0), inc=sa)
    p.emit("vector", lambda v: v.scalar_tensor_tensor(
        B1[:, :], B1[:, :], 0.0, B2[:, :], ALU.max, ALU.add),
        waits=[(sa, sa.n)], inc=sv)
    p.emit("vector", lambda v: v.tensor_scalar_mul(
        B2[:, :], B1[:, :], Acol), waits=[(sv, sv.n)], inc=sv)
    p.emit("scalar", lambda x_: x_.activation(
        B1[:, :], B2[:, :], AF.Exp), waits=[(sv, sv.n)], inc=sa)
    p.wait("vector", sa)
    for tt in range(NTT):
        sl = slice(TT * tt, TT * (tt + 1))
        p.emit("vector", lambda v, sl=sl: v.tensor_scalar(
            p.scr0[0:64, :], B1[:, sl], -1.0, -1.0, ALU.add, ALU.mult),
            inc=sv)
        p.emit("vector", lambda v, sl=sl: v.tensor_scalar_mul(
            p.scr0[0:64, :], p.scr0[0:64, :], scc[:, 10:11]), inc=sv)
        p.emit("vector", lambda v, sl=sl: v.tensor_mul(
            B0[:, sl], B0[:, sl], p.scr0[0:64, :]), inc=sv)
    p.emit("vector", lambda v: v.tensor_tensor_scan(
        B3[:, :], B1[:, :], B0[:, :], 0.0, ALU.mult, ALU.add),
        waits=[(sv, sv.n)], inc=sv)
    p.emit("gpsimd", lambda g: g.memset(B1[:, :], 0.0), waits=[(sv, sv.n)], inc=sg)
    p.wait("vector", sg)
    p.emit("vector", lambda v: v.tensor_tensor_scan(
        B0[:, :], B2[:, :], B1[:, :], 0.0, ALU.add, ALU.add), inc=sv)
    p.emit("scalar", lambda x_: x_.activation(
        B0[:, :], B0[:, :], AF.Exp), waits=[(sv, sv.n)], inc=sa)
    p.wait("sync", sa)
    p.wait("vector", sa)
    pay = p.idram("l0a_pay", [2, 64])
    ago = p.idram("l0a_ago", [8, 64])
    p.emit("sync", lambda s: s.dma_start(
        out=pay[0:1, :].rearrange("a b -> b a"), in_=B0[:, 2047:2048],
        allow_slow_non_contiguous=True), inc=sy, k=16)
    p.emit("sync", lambda s: s.dma_start(
        out=pay[1:2, :].rearrange("a b -> b a"), in_=B3[:, 2047:2048],
        allow_slow_non_contiguous=True), inc=sy, k=16)
    p.wait("gpsimd", sy)
    p.emit("gpsimd", lambda g: g.collective_compute(
        "AllGather", ALU.bypass, replica_groups=GROUPS,
        ins=[pay.opt()], outs=[ago.opt()]), inc=cc)
    p.emit("gpsimd", lambda g: g.dma_start(
        out=scc[:, 0:8], in_=ago.rearrange("r s -> s r"),
        allow_slow_non_contiguous=True), waits=[(cc, cc.n)], inc=sg, k=16)
    p.wait("vector", sg)
    p.emit("vector", lambda v: v.memset(scc[:, 11:12], 0.0), inc=sv)
    for r in range(3):
        p.emit("vector", lambda v, r=r: v.scalar_tensor_tensor(
            scc[:, 11:12], scc[:, 11:12], scc[:, 2 * r:2 * r + 1],
            scc[:, 2 * r + 1:2 * r + 2], ALU.mult, ALU.add),
            waits=[(sv, sv.n)], inc=sv)
        p.emit("vector", lambda v, r=r: v.tensor_copy(
            scc[:, 12 + r:13 + r], scc[:, 11:12]), waits=[(sv, sv.n)], inc=sv)
    p.emit("vector", lambda v: v.tensor_scalar_mul(
        scc[:, 15:16], scc[:, 12:13], selc[0:64, 0:1]), waits=[(sv, sv.n)], inc=sv)
    for r in range(1, 3):
        p.emit("vector", lambda v, r=r: v.scalar_tensor_tensor(
            scc[:, 15:16], scc[:, 12 + r:13 + r], selc[0:64, r:r + 1],
            scc[:, 15:16], ALU.mult, ALU.add), waits=[(sv, sv.n)], inc=sv)
    p.emit("vector", lambda v: v.scalar_tensor_tensor(
        B3[:, :], B0[:, :], scc[:, 15:16], B3[:, :], ALU.mult, ALU.add),
        waits=[(sv, sv.n)], inc=sv)
    p.emit("vector", lambda v: v.tensor_copy(zb[:, :], B3[:, :]),
           waits=[(sv, sv.n)], inc=sv)
    wz = [(sv, sv.n)]
    svo = sv.n
    for oc in range(NCH):
        for tt in range(NTT):
            sl = slice(TT * tt, TT * (tt + 1))
            i = oc * NTT + tt
            bk = i % 4
            ps = p.pb[bk]
            wps = [(sv, svo + i - 3)] if i >= 4 else wz
            p.emit("tensor", lambda t, oc=oc, sl=sl, ps=ps: t.matmul(
                ps[:, :], M[:, 128 * oc:128 * (oc + 1)], zb[:, sl],
                start=True, stop=True), waits=wps, inc=sp)
            p.emit("vector", lambda v, oc=oc, sl=sl, ps=ps: v.scalar_tensor_tensor(
                p.x8[oc][:, sl], ps[:, :], bout[:, oc:oc + 1], p.x8[oc][:, sl],
                ALU.add, ALU.add), waits=[(sp, sp.n)], inc=sv)
    p.barrier()


def prep_l0_attn(inp, core=None):
    winT = _t(inp["L0_attn_W_in"])
    wx = np.asarray(inp["L0_attn_W_x"], np.float32)
    wdt = np.asarray(inp["L0_attn_W_dt"], np.float32)
    wxdt = np.concatenate([wx, wdt], 0).T
    dT = _t(inp["L0_attn_D_mat"])
    woT = _t(inp["L0_attn_W_out"])
    def chunkcat(Mx, kc, m):
        return np.ascontiguousarray(
            Mx.reshape(kc, 128, m).transpose(1, 0, 2).reshape(128, kc * m))
    return {
        "winB": host_blocks(winT, 256),
        "wxdtC": chunkcat(wxdt, 16, 128),
        "dTC": chunkcat(dT, 16, 64),
        "woB": host_blocks(woT, 512),
        "binc": _colmajor(inp["L0_attn_b_in"], 16),
        "convw": np.ascontiguousarray(
            np.asarray(inp["L0_attn_conv_w"], np.float32).reshape(2048, 4)
            .reshape(16, 128, 4).transpose(1, 0, 2).reshape(128, 64)),
        "convb": _colmajor(inp["L0_attn_conv_b"], 16),
        "bout": _colmajor(inp["L0_attn_b_out"], 8),
        "bxdt": np.stack([np.asarray(inp["L0_attn_b_x"], np.float32),
                          np.asarray(inp["L0_attn_b_dt"], np.float32)], 1),
        "alog": np.asarray(inp["L0_attn_A_log"], np.float32).reshape(64, 1),
    }


def percore_l0_attn(x_full, core):
    b, q = divmod(core, 4)
    sel = np.zeros((64, 4), np.float32)
    if q > 0:
        sel[:, q - 1] = 1.0
    hv = np.ones((128, 1), np.float32) if q > 0 else np.zeros((128, 1), np.float32)
    if q > 0:
        halo = np.asarray(x_full[b, q * T - 3:q * T, :], np.float32)
    else:
        halo = np.zeros((3, 1024), np.float32)
    xh = np.ascontiguousarray(halo.T.reshape(8, 128, 3).transpose(1, 0, 2)
                              .reshape(128, 24))
    return {"l0sel": sel, "hval": hv, "xh": xh}


L0ATTN_SHAPES = {
    "winB": (8, 128, 2048), "wxdtC": (128, 2048), "dTC": (128, 1024),
    "woB": (2, 128, 8192), "binc": (128, 16), "convw": (128, 64),
    "convb": (128, 16), "bout": (128, 8), "bxdt": (64, 2), "alog": (64, 1),
    "l0sel": (64, 4), "hval": (128, 1), "xh": (128, 24),
}



def emit_l3_attn(p, W):
    nc = p.nc
    emit_ln(p, "l3a_ln")
    sp = p.sem("l3a_p")
    sa = p.sem("l3a_a")
    sv = p.sem("l3a_v")
    sy = p.sem("l3a_y")
    sg = p.sem("l3a_g")
    cc1 = p.sem("l3a_cc1")
    cc2 = p.sem("l3a_cc2")
    cc3 = p.sem("l3a_cc3")
    ring = WRing(p, "l3a")
    p.pool_reset()
    bqc = load_small(p, W["bqc"], 128, 8)
    bkc = load_small(p, W["bkc"], 128, 8)
    bvr = load_small(p, W["bvr"], 1, 1024, BF16)
    delta = load_small(p, W["delta"], 128, 256)
    rows = p.pallocF(128, 20)
    emb = p.pallocB(128, 512)
    etb = p.pallocB(128, 512)
    wait_small(p)

    iota_np = (np.arange(512)[None, :] - np.arange(128)[:, None]).astype(np.float32)
    iota_d = nc.inline_tensor(iota_np, "l3a_iota").ap()
    iotaI = p.scr1
    p.emit("sync", lambda s_: s_.dma_start(out=iotaI[:, :], in_=iota_d), inc=sy, k=16)

    payK = p.idram("l3a_payK", [1024, T], BF16)
    payV = p.idram("l3a_payV", [T, 1024], BF16)
    agK = p.idram("l3a_agK", [4096, T], BF16)
    agV = p.idram("l3a_agV", [4 * T, 1024], BF16)
    payS = p.idram("l3a_payS", [1, 2], F32)
    agS = p.idram("l3a_agS", [4, 2], F32)
    ybounce = p.idram("l3a_yb", [T, 1024], BF16)

    qC, kC, vC = W["wqB"], W["wkB"], W["wvC"]
    qt8 = [p.bigA[:, 2048 * c:2048 * (c + 1)] for c in range(NCH)]
    wvt = p.bigB[:, 0:8192]
    kstage = p.bigB[:, 8192:10240]
    vstage = p.bigB[:, 10240:11264]
    p.emit("gpsimd", lambda g: g.dma_start(out=wvt, in_=W["wvC"]), inc=sg, k=16)

    for oc in range(NCH):
        g = ring.load(W["wqB"][oc], 1)
        for tt in range(NTT):
            sl = slice(TT * tt, TT * (tt + 1))
            i = oc * NTT + tt
            ps = p.pb[i % 4]
            wps = [(sa, sa.n - 3)] if i >= 4 else [(sv, sv.n)]
            for ic in range(NCH):
                p.emit("tensor", lambda t, ic=ic, oc=oc, g=g, sl=sl, ps=ps: t.matmul(
                    ps[:, :], ring.view(g, ic * 128, 128), p.lnx8[ic][:, sl],
                    start=(ic == 0), stop=(ic == NCH - 1)),
                    waits=([(ring.ld, ring.wv(g))] + wps) if ic == 0 else [],
                    inc=sp if ic == NCH - 1 else None)
            p.emit("scalar", lambda x_, oc=oc, sl=sl, ps=ps: x_.add(
                qt8[oc][:, sl], ps[:, :], bqc[:, oc:oc + 1]),
                waits=[(sp, sp.n)], inc=sa)
        ring.mark_free(g, sp, sp.n)
    kst_done = 0
    for oc in range(NCH):
        g = ring.load(W["wkB"][oc], 1)
        for tt in range(NTT):
            sl = slice(TT * tt, TT * (tt + 1))
            i = oc * NTT + tt
            ps = p.pb[i % 4]
            wps = [(sa, sa.n - 3)]
            for ic in range(NCH):
                p.emit("tensor", lambda t, ic=ic, oc=oc, g=g, sl=sl, ps=ps: t.matmul(
                    ps[:, :], ring.view(g, ic * 128, 128), p.lnx8[ic][:, sl],
                    start=(ic == 0), stop=(ic == NCH - 1)),
                    waits=([(ring.ld, ring.wv(g)), (sy, kst_done)] + wps)
                          if ic == 0 else [],
                    inc=sp if ic == NCH - 1 else None)
            p.emit("scalar", lambda x_, oc=oc, sl=sl, ps=ps: x_.add(
                kstage[:, sl], ps[:, :], bkc[:, oc:oc + 1]),
                waits=[(sp, sp.n)], inc=sa)
        p.emit("sync", lambda s_, oc=oc: s_.dma_start(
            out=payK[128 * oc:128 * (oc + 1), :], in_=kstage[:, :]),
            waits=[(sa, sa.n)], inc=sy, k=16)
        kst_done = sy.n
        ring.mark_free(g, sp, sp.n)
    p.wait("tensor", sg)
    vst_done = 0
    for q in range(T // 128):
        ql = slice(128 * q, 128 * (q + 1))
        for half in range(2):
            ps = p.pb[4 + half]
            wv = [(sy, vst_done)]
            for ic in range(NCH):
                p.emit("tensor", lambda t, ic=ic, ql=ql, half=half, ps=ps: t.matmul(
                    ps[:, :], p.lnx8[ic][:, ql],
                    wvt[:, 1024 * ic + 512 * half:1024 * ic + 512 * (half + 1)],
                    start=(ic == 0), stop=False),
                    waits=wv if ic == 0 else [])
            p.emit("tensor", lambda t, half=half, ps=ps: t.matmul(
                ps[:, :], p.ones_row[0:1, :], bvr[0:1, 512 * half:512 * (half + 1)],
                start=False, stop=True), inc=sp)
            p.emit("vector", lambda v, half=half, ps=ps: v.tensor_copy(
                vstage[:, 512 * half:512 * (half + 1)], ps[:, :]),
                waits=[(sp, sp.n)], inc=sv)
        p.emit("sync", lambda s_, q=q, ql=ql: s_.dma_start(
            out=payV[ql, :], in_=vstage[:, :]), waits=[(sv, sv.n)], inc=sy, k=16)
        vst_done = sy.n
    p.wait("sync", sy)
    p.barrier()

    p.emit("gpsimd", lambda g: g.collective_compute(
        "AllGather", ALU.bypass, replica_groups=GROUPS,
        ins=[payK.opt()], outs=[agK.opt()]), inc=cc1)
    p.emit("gpsimd", lambda g: g.collective_compute(
        "AllGather", ALU.bypass, replica_groups=GROUPS,
        ins=[payV.opt()], outs=[agV.opt()]), inc=cc2)
    p.wait("sync", cc1)
    p.wait("sync", cc2)
    p.wait("scalar", cc1)
    p.wait("scalar", cc2)

    kbuf = [p.bigB[:, 0:4096], p.bigB[:, 4096:8192]]
    vbuf = [p.bigB[:, 8192:12288], p.bigB[:, 12288:16384]]
    agK4 = agK.rearrange("(r p) k -> r p k", p=1024)
    agV4 = agV.rearrange("(s p) d -> s p d", p=128)
    syc = p.sem("l3a_syc")
    buffree = {}
    for q in range(T // 128):
        ql = slice(128 * q, 128 * (q + 1))
        psY0, psY1 = p.pb[4], p.pb[5]
        p.emit("vector", lambda v, q=q: v.memset(rows[:, q:q + 1], 0.0), inc=sv)
        for kt in range(16):
            r, ktl = divmod(kt, 4)
            buf = kbuf[kt % 2]
            vbf = vbuf[kt % 2]
            gi = q * 16 + kt
            wfree = [buffree[gi % 2]] if gi % 2 in buffree and gi >= 2 else []
            p.emit("sync", lambda s_, r=r, ktl=ktl, buf=buf: s_.dma_start(
                out=buf[:, :].rearrange("p (c k) -> p c k", c=8),
                in_=agK4[r].rearrange("(c pp) k -> pp c k", pp=128)
                    [:, :, 512 * ktl:512 * (ktl + 1)]),
                waits=wfree, inc=sy, k=16)
            p.emit("scalar", lambda x_, kt=kt, vbf=vbf: x_.dma_start(
                out=vbf[:, :].rearrange("p (s d) -> p s d", s=4),
                in_=agV4.rearrange("(t s) p d -> t p s d", s=4)[kt]),
                waits=wfree, inc=syc, k=16)
            psL = p.pb[0]
            for ic in range(NCH):
                p.emit("tensor", lambda t, ic=ic, q=q, buf=buf, psL=psL: t.matmul(
                    psL[:, :], qt8[ic][:, 128 * q:128 * (q + 1)],
                    buf[:, 512 * ic:512 * (ic + 1)],
                    start=(ic == 0), stop=(ic == NCH - 1)),
                    waits=[(sy, sy.n), (syc, syc.n), (sv, sv.n)] if ic == 0 else [],
                    inc=sp if ic == NCH - 1 else None)
            pair = q * 16 + kt
            p.emit("vector", lambda v, pair=pair: v.tensor_scalar(
                p.scrE[:, :], iotaI[:, :], delta[:, pair:pair + 1], 1.0,
                ALU.is_le, ALU.mult), waits=[(sv, sv.n)], inc=sv)
            p.emit("scalar", lambda x_, psL=psL: x_.activation(
                p.scr0[:, :], psL[:, :], AF.Exp, scale=1.0 / 32),
                waits=[(sp, sp.n)], inc=sa)
            p.emit("vector", lambda v: v.tensor_mul(
                emb[:, :], p.scr0[:, :], p.scrE[:, :]),
                waits=[(sa, sa.n), (sv, sv.n)], inc=sv)
            p.emit("vector", lambda v: v.tensor_mul(
                p.scr0[:, :], p.scr0[:, :], p.scrE[:, :]),
                waits=[(sv, sv.n)], inc=sv)
            p.emit("vector", lambda v: v.reduce_sum(
                rows[:, 16:17], p.scr0[:, :], AX.X), waits=[(sv, sv.n)], inc=sv)
            p.emit("vector", lambda v, q=q: v.tensor_add(
                rows[:, q:q + 1], rows[:, q:q + 1], rows[:, 16:17]),
                waits=[(sv, sv.n)], inc=sv)
            for j in range(4):
                p.emit("tensor", lambda t, j=j: t.transpose(
                    p.pt[0:128, 128 * j:128 * (j + 1)], emb[:, 128 * j:128 * (j + 1)],
                    p.ident[:, :]), waits=[(sv, sv.n)] if j == 0 else [], inc=sp)
            p.emit("vector", lambda v: v.tensor_copy(
                etb[:, :], p.pt[0:128, 0:512]), waits=[(sp, sp.n)], inc=sv)
            for sub in range(4):
                for half in range(2):
                    psY = psY0 if half == 0 else psY1
                    p.emit("tensor", lambda t, sub=sub, half=half, vbf=vbf, psY=psY:
                        t.matmul(
                            psY[:, :], etb[:, 128 * sub:128 * (sub + 1)],
                            vbf[:, 1024 * sub + 512 * half:1024 * sub + 512 * (half + 1)],
                            start=(kt == 0 and sub == 0), stop=(kt == 15 and sub == 3)),
                        waits=[(sv, sv.n)] if (sub == 0 and half == 0) else [],
                        inc=sp if (sub == 3 and half == 1) else None)
            buffree[gi % 2] = (sp, sp.n)
        p.emit("vector", lambda v, q=q: v.tensor_copy(
            emb[:, :], psY0[:, :]), waits=[(sp, sp.n)], inc=sv)
        p.emit("vector", lambda v, q=q: v.tensor_copy(
            etb[:, :], psY1[:, :]), inc=sv)
        p.emit("sync", lambda s_, q=q, ql=ql: s_.dma_start(
            out=ybounce[ql, 0:512], in_=emb[:, :]), waits=[(sv, sv.n)], inc=sy, k=16)
        p.emit("sync", lambda s_, q=q, ql=ql: s_.dma_start(
            out=ybounce[ql, 512:1024], in_=etb[:, :]), inc=sy, k=16)
    p.wait("sync", sy)
    p.barrier()

    p.emit("vector", lambda v: v.tensor_copy(
        emb[:, 0:16], rows[:, 0:16]), inc=sv)
    p.emit("tensor", lambda t: t.matmul(
        p.pb[0][0:1, 0:16], p.ones_col[:, :], emb[:, 0:16],
        start=True, stop=True), waits=[(sv, sv.n)], inc=sp)
    p.emit("vector", lambda v: v.reduce_sum(
        rows[0:1, 17:18], p.pb[0][0:1, 0:16], AX.X), waits=[(sp, sp.n)], inc=sv)
    p.emit("sync", lambda s_: s_.dma_start(
        out=payS[0:1, 0:1], in_=rows[0:1, 17:18]), waits=[(sv, sv.n)], inc=sy, k=16)
    p.wait("gpsimd", sy)
    p.emit("gpsimd", lambda g: g.collective_compute(
        "AllGather", ALU.bypass, replica_groups=GROUPS,
        ins=[payS.opt()], outs=[agS.opt()]), inc=cc3)
    p.emit("gpsimd", lambda g: g.dma_start(
        out=rows[0:4, 18:19], in_=agS[:, 0:1],
        allow_slow_non_contiguous=True), waits=[(cc3, cc3.n)], inc=sg, k=16)
    p.wait("vector", sg)
    p.emit("vector", lambda v: v.tensor_copy(
        emb[0:4, 0:1], rows[0:4, 18:19]), waits=[(sv, sv.n)], inc=sv)
    p.emit("tensor", lambda t: t.matmul(
        p.pb[0][0:1, 0:1], p.ones_col[0:4, :], emb[0:4, 0:1],
        start=True, stop=True), waits=[(sv, sv.n)], inc=sp)
    p.emit("vector", lambda v: v.reciprocal(
        rows[0:1, 19:20], p.pb[0][0:1, 0:1]), waits=[(sp, sp.n)], inc=sv)
    p.emit("vector", lambda v: v.tensor_copy(
        p.rowBb[0:1, 0:1], rows[0:1, 19:20]), waits=[(sv, sv.n)], inc=sv)
    p.emit("tensor", lambda t: t.matmul(
        p.pb[1][:, 0:1], p.ones_row[:, :], p.rowBb[0:1, 0:1],
        start=True, stop=True), waits=[(sv, sv.n)], inc=sp)
    p.emit("vector", lambda v: v.tensor_copy(
        rows[:, 16:17], p.pb[1][:, 0:1]), waits=[(sp, sp.n)], inc=sv)
    for q in range(T // 128):
        ql = slice(128 * q, 128 * (q + 1))
        p.emit("sync", lambda s_, q=q, ql=ql: s_.dma_start(
            out=emb[:, 0:512], in_=ybounce[ql, 0:512]),
            waits=[(sv, sv.n)], inc=sy, k=16)
        p.emit("sync", lambda s_, q=q, ql=ql: s_.dma_start(
            out=etb[:, 0:512], in_=ybounce[ql, 512:1024]), inc=sy, k=16)
        p.wait("tensor", sy)
        for oc in range(NCH):
            src = emb if oc < 4 else etb
            p.emit("tensor", lambda t, oc=oc, src=src: t.transpose(
                p.pt[0:128, 0:128], src[:, 128 * (oc % 4):128 * (oc % 4 + 1)],
                p.ident[:, :]), waits=[(sv, sv.n)], inc=sp)
            p.emit("vector", lambda v, oc=oc, ql=ql: v.scalar_tensor_tensor(
                p.x8[oc][:, ql], p.pt[0:128, 0:128], rows[:, 16:17],
                p.x8[oc][:, ql], ALU.mult, ALU.add), waits=[(sp, sp.n)], inc=sv)
    p.barrier()


def prep_l3_attn(inp):
    Wq = np.asarray(inp["L3_attn_Wq"], np.float32)
    Wk = np.asarray(inp["L3_attn_Wk"], np.float32)
    Wv = np.asarray(inp["L3_attn_Wv"], np.float32)
    def chunkcat(Mx, kc, m):
        return np.ascontiguousarray(
            Mx.reshape(kc, 128, m).transpose(1, 0, 2).reshape(128, kc * m))
    return {
        "wqB": host_blocks(Wq, 128),
        "wkB": host_blocks(Wk, 128),
        "wvC": chunkcat(Wv, 8, 1024),
        "bqc": _colmajor(inp["L3_attn_bq"], 8),
        "bkc": _colmajor(inp["L3_attn_bk"], 8),
        "bvr": np.asarray(inp["L3_attn_bv"], np.float32).reshape(1, 1024),
    }


def percore_l3_attn(x_full, core):
    b, q = divmod(core, 4)
    delta = np.zeros((128, 256), np.float32)
    for qt in range(16):
        for kt in range(16):
            delta[:, qt * 16 + kt] = q * 2048 + qt * 128 - kt * 512
    return {"delta": delta}


L3ATTN_SHAPES = {
    "wqB": (8, 128, 1024), "wkB": (8, 128, 1024), "wvC": (128, 8192),
    "bqc": (128, 8), "bkc": (128, 8), "bvr": (1, 1024), "delta": (128, 256),
}




def emit_l1_attn(p, W, dbg=None):
    nc = p.nc
    emit_ln(p, "l1a_ln")
    sp = p.sem("l1a_p")
    sa = p.sem("l1a_a")
    sv = p.sem("l1a_v")
    sy = p.sem("l1a_y")
    sg = p.sem("l1a_g")
    cc = p.sem("l1a_cc")
    ring = WRing(p, "l1a")
    p.pool_reset()
    selPN = load_small(p, W["selPN"], 128, 8)
    pnv = load_small(p, W["pnv"], 128, 2)
    dcol = p.pallocF(128, 2)
    wait_small(p)
    maskP = p.scrE[:, 0:128]
    maskN = p.scrE[:, 128:256]
    maskP0 = p.scrE[:, 256:384]
    maskN15 = p.scrE[:, 384:512]
    mpT_d = nc.inline_tensor(np.ascontiguousarray(
        (np.arange(128)[:, None] >= np.arange(128)[None, :] + 1)
        .astype(np.float32)), "l1a_mpT").ap()
    mnT_d = nc.inline_tensor(np.ascontiguousarray(
        (np.arange(128)[:, None] <= np.arange(128)[None, :] - 1)
        .astype(np.float32)), "l1a_mnT").ap()
    p.emit("sync", lambda s_: s_.dma_start(out=maskP, in_=mpT_d), inc=sy, k=16)
    p.emit("sync", lambda s_: s_.dma_start(out=maskN, in_=mnT_d), inc=sy, k=16)
    p.wait("vector", sy)
    p.emit("vector", lambda v: v.tensor_scalar_mul(maskP0, maskP, pnv[:, 0:1]),
           inc=sv)
    p.emit("vector", lambda v: v.tensor_scalar_mul(maskN15, maskN, pnv[:, 1:2]),
           inc=sv)

    pay = p.idram("l1a_pay", [128, 3072], BF16)
    ago = p.idram("l1a_ago", [512, 3072], BF16)
    gfm = p.bigB[:, 0:1024]
    def _gred(v, c):
        with p.nc.allow_low_precision(reason="16-elt block sum, fp32 internal"):
            return v.tensor_reduce(
                gfm[:, 128 * c:128 * (c + 1)],
                p.lnx8[c][:, :].rearrange("p (g s) -> p g s", s=16),
                AX.X, ALU.add)
    for c in range(NCH):
        p.emit("vector", lambda v, c=c: _gred(v, c), inc=sv)
    p.wait("sync", sv)
    p.emit("sync", lambda s_: s_.dma_start(out=pay[:, 0:1024], in_=gfm), inc=sy, k=16)
    for c in range(NCH):
        p.emit("sync", lambda s_, c=c: s_.dma_start(
            out=pay[:, 1024 + 128 * c:1024 + 128 * (c + 1)],
            in_=p.lnx8[c][:, 0:128]), inc=sy, k=16)
        p.emit("sync", lambda s_, c=c: s_.dma_start(
            out=pay[:, 2048 + 128 * c:2048 + 128 * (c + 1)],
            in_=p.lnx8[c][:, T - 128:T]), inc=sy, k=16)
    p.wait("sync", sy)
    p.wait("gpsimd", sy)
    p.emit("gpsimd", lambda g: g.collective_compute(
        "AllGather", ALU.bypass, replica_groups=GROUPS,
        ins=[pay.opt()], outs=[ago.opt()]), inc=cc)
    ago4 = ago.rearrange("(r pp) x -> r pp x", pp=128)
    sideg = p.bigB[:, 0:4096]
    hbuf = [p.bigB[:, 4096 + 1024 * i:4096 + 1024 * (i + 1)] for i in range(4)]
    haloP = p.bigB[:, 11264 + 0:11264 + 1024]
    haloN = p.bigB[:, 12288:13312]
    for r_ in range(4):
        p.emit("gpsimd", lambda g, r_=r_: g.dma_start(
            out=sideg.rearrange("p (c r k) -> p c r k", c=8, r=4)[:, :, r_, :],
            in_=ago4[r_][:, 0:1024].rearrange("pp (c k) -> pp c k", c=8)),
            waits=[(cc, cc.n)] if r_ == 0 else [], inc=sg, k=16)
    for r in range(4):
        p.emit("gpsimd", lambda g, r=r: g.dma_start(
            out=hbuf[r], in_=ago4[r][:, 2048:3072]), inc=sg, k=16)
    p.wait("vector", sg)
    for r in range(4):
        if r == 0:
            p.emit("vector", lambda v, r=r: v.tensor_scalar_mul(
                haloP, hbuf[r], selPN[:, 0:1]), waits=[(sv, sv.n)], inc=sv)
        else:
            p.emit("vector", lambda v, r=r: v.scalar_tensor_tensor(
                haloP, hbuf[r], selPN[:, r:r + 1], haloP, ALU.mult, ALU.add),
                waits=[(sv, sv.n)], inc=sv)
    for r in range(4):
        p.emit("gpsimd", lambda g, r=r: g.dma_start(
            out=hbuf[r], in_=ago4[r][:, 1024:2048]),
            waits=[(sv, sv.n)] if r == 0 else [], inc=sg, k=16)
    p.wait("vector", sg)
    for r in range(4):
        if r == 0:
            p.emit("vector", lambda v, r=r: v.tensor_scalar_mul(
                haloN, hbuf[r], selPN[:, 4:5]), waits=[(sv, sv.n)], inc=sv)
        else:
            p.emit("vector", lambda v, r=r: v.scalar_tensor_tensor(
                haloN, hbuf[r], selPN[:, 4 + r:5 + r], haloN, ALU.mult, ALU.add),
                waits=[(sv, sv.n)], inc=sv)
    p.barrier()
    if dbg == "halo":
        return

    qg = p.bigA[:, 0:4096]
    kg = p.bigA[:, 4096:8192]
    vtok = p.bigA[:, 8192:12352]
    expb = p.bigB[:, 4096:4992]
    sidek = p.bigB[:, 4992:6016]
    svtok = p.bigB[:, 6016:7056]
    khp = p.bigB[:, 7056:7312]
    khn = p.bigB[:, 7312:7568]
    vhp = p.bigB[:, 7568:7828]
    vhn = p.bigB[:, 7828:8088]
    ysc4 = p.bigB[:, 8088:8344]
    yfm2 = [p.bigB[:, 8344:10392], p.bigB[:, 13312:15360]]

    p.emit("gpsimd", lambda g: g.memset(vtok, 1.0), inc=sv)
    p.emit("gpsimd", lambda g: g.memset(svtok, 1.0), inc=sv)
    p.emit("gpsimd", lambda g: g.memset(vhp, 1.0), inc=sv)
    p.emit("gpsimd", lambda g: g.memset(vhn, 1.0), inc=sv)
    p.wait("tensor", sv)
    p.wait("vector", sv)
    nhg = 2 if dbg == "wonly2" else 4
    for hg in range(nhg):
        gq = ring.load(W["wqB"][hg], 1)
        gk = ring.load(W["wkB"][hg], 1)
        gv = ring.load(W["wvB"][hg], 1)
        sv_hg = sv.n
        sa_hg = sa.n
        bank_free = {}
        for which, gw, dst in ((0, gq, qg), (1, gk, kg)):
            for j in range(2):
                for tt in range(NTT):
                    sl = slice(TT * tt, TT * (tt + 1))
                    i = which * 8 + j * NTT + tt
                    bk = i % 4
                    ps = p.pb[bk]
                    wps = ([bank_free[bk]] if bk in bank_free
                           else [(sv, sv_hg), (sa, sa_hg)])
                    for ic in range(NCH):
                        p.emit("tensor", lambda t, ic=ic, j=j, gw=gw, sl=sl, ps=ps: t.matmul(
                            ps[:, :], ring.view(gw, ic * 256 + j * 128, 128),
                            p.lnx8[ic][:, sl], start=(ic == 0), stop=(ic == NCH - 1)),
                            waits=([(ring.ld, ring.wv(gw))] + wps) if ic == 0 else [],
                            inc=sp if ic == NCH - 1 else None)
                    p.emit("scalar", lambda x_, j=j, sl=sl, dst=dst, ps=ps: x_.activation(
                        dst[:, 2048 * j:2048 * (j + 1)][:, sl], ps[:, :], AF.Copy),
                        waits=[(sp, sp.n)], inc=sa)
                    bank_free[bk] = (sa, sa.n)
        for nb in range(16):
            ql = slice(128 * nb, 128 * (nb + 1))
            ps = p.pb[4]
            for ic in range(NCH):
                p.emit("tensor", lambda t, ic=ic, gv=gv, ql=ql, ps=ps: t.matmul(
                    ps[:, 0:256], p.lnx8[ic][:, ql], ring.view(gv, ic * 256, 256),
                    start=(ic == 0), stop=(ic == NCH - 1)),
                    waits=([(ring.ld, ring.wv(gv)), (sv, sv.n)]) if ic == 0 else [],
                    inc=sp if ic == NCH - 1 else None)
            p.emit("vector", lambda v, nb=nb, ps=ps: v.tensor_copy(
                vtok[:, 260 * nb:260 * (nb + 1)].rearrange(
                    "p (h w) -> p h w", w=65)[:, :, 0:64],
                ps[:, 0:256].rearrange("p (h w) -> p h w", h=4)),
                waits=[(sp, sp.n)], inc=sv)
        for j in range(2):
            ps = p.pb[4]
            for ic in range(NCH):
                p.emit("tensor", lambda t, ic=ic, j=j, gk=gk, ps=ps: t.matmul(
                    ps[:, :], ring.view(gk, ic * 256 + j * 128, 128),
                    sideg[:, 512 * ic:512 * (ic + 1)],
                    start=(ic == 0), stop=(ic == NCH - 1)),
                    waits=[(sv, sv.n)] if ic == 0 else [],
                    inc=sp if ic == NCH - 1 else None)
            p.emit("vector", lambda v, j=j, ps=ps: v.tensor_copy(
                sidek[:, 512 * j:512 * (j + 1)], ps[:, :]),
                waits=[(sp, sp.n)], inc=sv)
        for st in range(4):
            ps = p.pb[4]
            for ic in range(NCH):
                p.emit("tensor", lambda t, ic=ic, st=st, gv=gv, ps=ps: t.matmul(
                    ps[:, 0:256],
                    sideg[:, 512 * ic + 128 * st:512 * ic + 128 * (st + 1)],
                    ring.view(gv, ic * 256, 256),
                    start=(ic == 0), stop=(ic == NCH - 1)),
                    waits=[(sv, sv.n)] if ic == 0 else [],
                    inc=sp if ic == NCH - 1 else None)
            p.emit("vector", lambda v, st=st, ps=ps: v.tensor_copy(
                svtok[:, 260 * st:260 * (st + 1)].rearrange(
                    "p (h w) -> p h w", w=65)[:, :, 0:64],
                ps[:, 0:256].rearrange("p (h w) -> p h w", h=4)),
                waits=[(sp, sp.n)], inc=sv)
        for j in range(2):
            for halo, dstk in ((haloP, khp), (haloN, khn)):
                ps = p.pb[4]
                for ic in range(NCH):
                    p.emit("tensor", lambda t, ic=ic, j=j, gk=gk, halo=halo, ps=ps: t.matmul(
                        ps[:, 0:128], ring.view(gk, ic * 256 + j * 128, 128),
                        halo[:, 128 * ic:128 * (ic + 1)],
                        start=(ic == 0), stop=(ic == NCH - 1)),
                        waits=[(sv, sv.n)] if ic == 0 else [],
                        inc=sp if ic == NCH - 1 else None)
                p.emit("vector", lambda v, j=j, dstk=dstk, ps=ps: v.tensor_copy(
                    dstk[:, 128 * j:128 * (j + 1)], ps[:, 0:128]),
                    waits=[(sp, sp.n)], inc=sv)
        for halo, dstv in ((haloP, vhp), (haloN, vhn)):
            ps = p.pb[4]
            for ic in range(NCH):
                p.emit("tensor", lambda t, ic=ic, gv=gv, halo=halo, ps=ps: t.matmul(
                    ps[:, 0:256], halo[:, 128 * ic:128 * (ic + 1)],
                    ring.view(gv, ic * 256, 256),
                    start=(ic == 0), stop=(ic == NCH - 1)),
                    waits=[(sv, sv.n)] if ic == 0 else [],
                    inc=sp if ic == NCH - 1 else None)
            p.emit("vector", lambda v, dstv=dstv, ps=ps: v.tensor_copy(
                dstv[:, 0:260].rearrange("p (h w) -> p h w", w=65)[:, :, 0:64],
                ps[:, 0:256].rearrange("p (h w) -> p h w", h=4)),
                waits=[(sp, sp.n)], inc=sv)
        ring.mark_free(gq, sp, sp.n)
        ring.mark_free(gk, sp, sp.n)
        ring.mark_free(gv, sp, sp.n)
        p.wait("tensor", sv)
        if dbg == "proj":
            p.barrier()
            return
        skip_wo = (dbg == "projall")

        for nb in range(0 if dbg in ("wonly", "wonly2", "projall") else 16):
            for h in range(4):
                hb = (h % 2) * 64
                jc = h // 2
                qv = qg[:, 2048 * jc:2048 * (jc + 1)][hb:hb + 64,
                                                      128 * nb:128 * (nb + 1)]
                kgj = kg[:, 2048 * jc:2048 * (jc + 1)]
                wq0 = [(sa, sa.n), (sv, sv.n)]
                lhs_prev = (khp[hb:hb + 64, 128 * jc:128 * jc + 128] if nb == 0
                            else kgj[hb:hb + 64, 128 * (nb - 1):128 * nb])
                lhs_next = (khn[hb:hb + 64, 128 * jc:128 * jc + 128] if nb == 15
                            else kgj[hb:hb + 64, 128 * (nb + 1):128 * (nb + 2)])
                lhs7 = [lhs_prev, kgj[hb:hb + 64, 128 * nb:128 * (nb + 1)], lhs_next]
                lhs7 += [sidek[hb:hb + 64, 512 * jc + 128 * st:512 * jc + 128 * (st + 1)]
                         for st in range(4)]
                n7 = 3 if dbg == "noside" else 7
                for i7 in range(n7):
                    out7 = (p.pb[0][:, 128 * i7:128 * (i7 + 1)] if i7 < 4
                            else p.pb[1][:, 128 * (i7 - 4):128 * (i7 - 3)])
                    p.emit("tensor", lambda t, i7=i7, lhs7=lhs7, qv=qv, out7=out7:
                        t.matmul(out7, lhs7[i7], qv, start=True, stop=True),
                        waits=wq0 if i7 == 0 else [], inc=sp if i7 == n7 - 1 else None)
                p.emit("scalar", lambda x_: x_.activation(
                    expb[:, 0:512], p.pb[0][:, :], AF.Exp, scale=0.125),
                    waits=[(sp, sp.n)], inc=sa)
                p.emit("scalar", lambda x_: x_.activation(
                    expb[:, 512:896], p.pb[1][:, 0:384], AF.Exp, scale=0.125),
                    inc=sa)
                mP = maskP0 if nb == 0 else maskP
                mN = maskN15 if nb == 15 else maskN
                p.emit("vector", lambda v, mP=mP: v.tensor_mul(
                    expb[:, 0:128], expb[:, 0:128], mP), waits=[(sa, sa.n)], inc=sv)
                p.emit("vector", lambda v, mN=mN: v.tensor_mul(
                    expb[:, 256:384], expb[:, 256:384], mN), inc=sv)
                if dbg == "noside":
                    p.emit("scalar", lambda x_: x_.memzero(expb[:, 384:896]))
                rhs7 = [vhp[:, :] if nb == 0 else vtok[:, 260 * (nb - 1):260 * nb],
                        vtok[:, 260 * nb:260 * (nb + 1)],
                        vhn[:, :] if nb == 15 else vtok[:, 260 * (nb + 1):260 * (nb + 2)],
                        svtok[:, 0:260], svtok[:, 260:520],
                        svtok[:, 520:780], svtok[:, 780:1040]]
                psY = p.pb[5]
                for i7 in range(7):
                    p.emit("tensor", lambda t, i7=i7, rhs7=rhs7, h=h, psY=psY: t.matmul(
                        psY[:, 0:65], expb[:, 128 * i7:128 * (i7 + 1)],
                        rhs7[i7][:, 65 * h:65 * (h + 1)],
                        start=(i7 == 0), stop=(i7 == 6)),
                        waits=[(sv, sv.n)] if i7 == 0 else [],
                        inc=sp if i7 == 6 else None)
                p.emit("vector", lambda v, psY=psY: v.tensor_scalar_add(
                    dcol[:, 0:1], psY[:, 64:65], 1.0), waits=[(sp, sp.n)], inc=sv)
                p.emit("vector", lambda v: v.reciprocal(
                    dcol[:, 1:2], dcol[:, 0:1]), waits=[(sv, sv.n)], inc=sv)
                p.emit("vector", lambda v, h=h, psY=psY: v.tensor_scalar_mul(
                    ysc4[:, 64 * h:64 * (h + 1)], psY[:, 0:64], dcol[:, 1:2]),
                    waits=[(sv, sv.n)], inc=sv)
            for j in range(2):
                p.emit("tensor", lambda t, j=j: t.transpose(
                    p.pt[0:128, 0:128], ysc4[:, 128 * j:128 * (j + 1)],
                    p.ident[:, :]), waits=[(sv, sv.n)], inc=sp)
                p.emit("vector", lambda v, j=j, nb=nb: v.tensor_copy(
                    yfm2[j][:, 128 * nb:128 * (nb + 1)],
                    p.pt[0:128, 0:128]), waits=[(sp, sp.n)], inc=sv)
        if dbg == "attn":
            p.barrier()
            return
        sv_wo = sv.n
        sa_wo = sa.n
        for oc in range(0 if skip_wo else NCH):
            g = ring.load(W["woB"][oc], 1)
            for tt in range(NTT):
                sl = slice(TT * tt, TT * (tt + 1))
                i = oc * NTT + tt
                bk = i % 4
                ps = p.pb[bk]
                wps = ([(sv, sv_wo + i - 3)] if i >= 4
                       else [(sv, sv_wo), (sa, sa_wo)])
                for j in range(2):
                    p.emit("tensor", lambda t, j=j, g=g, hg=hg, sl=sl, ps=ps: t.matmul(
                        ps[:, :], ring.view(g, (2 * hg + j) * 128, 128),
                        yfm2[j][:, sl],
                        start=(j == 0), stop=(j == 1)),
                        waits=([(ring.ld, ring.wv(g))] + wps) if j == 0 else [],
                        inc=sp if j == 1 else None)
                p.emit("vector", lambda v, oc=oc, sl=sl, ps=ps: v.tensor_add(
                    p.x8[oc][:, sl], p.x8[oc][:, sl], ps[:, :]),
                    waits=[(sp, sp.n)], inc=sv)
            ring.mark_free(g, sp, sp.n)
        p.wait("tensor", sv)
        if dbg == "wo":
            p.barrier()
            return
    p.barrier()


def prep_l1_attn(inp):
    Wq = np.asarray(inp["L1_attn_Wq"], np.float32).reshape(1024, 1024)
    Wk = np.asarray(inp["L1_attn_Wk"], np.float32).reshape(1024, 1024)
    Wv = np.asarray(inp["L1_attn_Wv"], np.float32).reshape(1024, 1024)
    Wo = np.asarray(inp["L1_attn_Wo"], np.float32).reshape(1024, 1024)
    def hgcat(Mx, hg):
        sl = Mx[:, 256 * hg:256 * (hg + 1)]
        return np.ascontiguousarray(
            sl.reshape(8, 128, 256).transpose(1, 0, 2).reshape(128, 2048))
    return {
        "wqB": np.stack([hgcat(Wq, hg) for hg in range(4)]),
        "wkB": np.stack([hgcat(Wk, hg) for hg in range(4)]),
        "wvB": np.stack([hgcat(Wv, hg) for hg in range(4)]),
        "woB": host_blocks(Wo, 128),
    }


def percore_l1_attn(x_full, core):
    b, q = divmod(core, 4)
    selPN = np.zeros((128, 8), np.float32)
    if q > 0:
        selPN[:, q - 1] = 1.0
    if q < 3:
        selPN[:, 4 + q + 1] = 1.0
    pnv = np.zeros((128, 2), np.float32)
    pnv[:, 0] = 1.0 if q > 0 else 0.0
    pnv[:, 1] = 1.0 if q < 3 else 0.0
    return {"selPN": selPN, "pnv": pnv}


L1ATTN_SHAPES = {
    "wqB": (4, 128, 2048), "wkB": (4, 128, 2048), "wvB": (4, 128, 2048),
    "woB": (8, 128, 1024), "selPN": (128, 8), "pnv": (128, 2),
}




def emit_l2_attn(p, W, dbg=None):
    nc = p.nc

    def _dump(*pairs):
        for dst, src in pairs:
            p.emit("vector", lambda v, dst=dst, src=src: v.tensor_copy(dst, src),
                   inc=p.sems["l2a_v"])
        p.barrier()
    emit_ln(p, "l2a_ln")
    sp = p.sem("l2a_p")
    sa = p.sem("l2a_a")
    sv = p.sem("l2a_v")
    sy = p.sem("l2a_y")
    sg = p.sem("l2a_g")
    cc1 = p.sem("l2a_cc1")
    cc2 = p.sem("l2a_cc2")
    ring = WRing(p, "l2a")
    p.pool_reset()
    gbaC = load_small(p, W["gbaC"], 128, 384, BF16)
    dtbR = load_small(p, W["dtbR"], 1, 16, BF16)
    alogR = load_small(p, W["alogR"], 1, 16)
    selc = load_small(p, W["l2sel"], 128, 4)
    bcolf = p.pallocF(128, 8)
    hb4 = [p.pallocB(128, 8) for _ in range(4)]
    o64 = p.pallocF(128, 64)
    cols = p.pallocF(128, 6)
    Abc = p.pallocF(128, 16)
    dtR16 = p.pallocF(16, 4)
    wait_small(p)

    strictT = p.scrE[:, 0:128]
    onesq = p.scrE[:, 128:256]
    strictT_d = nc.inline_tensor(np.ascontiguousarray(
        np.where(np.arange(128)[:, None] < np.arange(128)[None, :],
                 0.0, -60000.0).astype(np.float32)),
        "l2a_strictT").ap()
    p.emit("sync", lambda s_: s_.dma_start(out=strictT, in_=strictT_d), inc=sy, k=16)
    p.emit("gpsimd", lambda g: g.memset(onesq, 1.0), inc=sg)

    bigBf = p.bigB[:, :].bitcast(F32)
    gbd = [bigBf[0:128, 48 * i:48 * (i + 1)] for i in range(16)]
    dcum = [bigBf[0:128, 768 + 16 * i:768 + 16 * (i + 1)] for i in range(16)]
    S8 = [bigBf[0:128, 1024 + 64 * t:1024 + 64 * (t + 1)] for t in range(8)]
    pfx = [bigBf[0:128, 1536 + 512 * j:1536 + 512 * (j + 1)] for j in range(3)]
    carry = bigBf[0:128, 3072:3584]
    offsF = bigBf[0:1, 3600:3856]
    gsum = bigBf[0:1, 3872:3888]
    dcb = [p.bigB[0:128, 8192 + 16 * i:8192 + 16 * (i + 1)] for i in range(16)]
    wk0 = p.bigB[0:128, 8448:9472]
    ofm = p.bigB[0:128, 9472:10496]
    eselT = p.bigB[0:16, 10496:11520]
    cbB = p.bigB[0:128, 11520:11648]
    intb = p.bigB[0:128, 11648:11776]
    wkts = p.bigB[0:128, 11776:11840]
    Sb = p.bigB[0:128, 11840:11904]
    triT = p.bigB[0:128, 11904:12032]
    e127 = p.bigB[0:128, 12032:12033]
    offsB = p.bigB[0:1, 12064:12320]
    carryB = p.bigB[0:128, 12320:12832]
    dcolB = p.bigB[0:16, 12864:12866]
    dlB = p.bigB[0:1, 13056:13312]
    vwT = p.bigA

    triT_d = nc.inline_tensor(np.ascontiguousarray(
        (np.arange(128)[:, None] <= np.arange(128)[None, :]).astype(BF)),
        "l2a_triT").ap()
    e127_d = nc.inline_tensor(
        (np.arange(128)[:, None] == 127).astype(BF), "l2a_e127").ap()
    esel_np = np.zeros((16, 8, 128), dtype=BF)
    for t_ in range(8):
        esel_np[2 * t_, t_, 0:64] = 1
        esel_np[2 * t_ + 1, t_, 64:128] = 1
    esel_d = nc.inline_tensor(np.ascontiguousarray(
        esel_np.reshape(16, 1024)), "l2a_esel").ap()
    p.emit("sync", lambda s_: s_.dma_start(out=triT, in_=triT_d), inc=sy, k=16)
    p.emit("sync", lambda s_: s_.dma_start(out=e127, in_=e127_d), inc=sy, k=16)
    p.emit("sync", lambda s_: s_.dma_start(out=eselT, in_=esel_d), inc=sy, k=16)

    payB = p.idram("l2a_payB", [128, 8], BF16)
    agoB = p.idram("l2a_agoB", [512, 8], BF16)
    payS = p.idram("l2a_payS", [128, 528], F32)
    agoS = p.idram("l2a_agoS", [512, 528], F32)

    for c in range(NCH):
        p.emit("sync", lambda s_, c=c: s_.dma_start(
            out=payB[:, c:c + 1], in_=p.lnx8[c][:, T - 1:T],
            allow_slow_non_contiguous=True), inc=sy, k=16)
    p.wait("gpsimd", sy)
    p.emit("gpsimd", lambda g: g.collective_compute(
        "AllGather", ALU.bypass, replica_groups=GROUPS,
        ins=[payB.opt()], outs=[agoB.opt()]), inc=cc1)
    for r in range(4):
        p.emit("gpsimd", lambda g, r=r: g.dma_start(
            out=hb4[r], in_=agoB[128 * r:128 * (r + 1), :]),
            waits=[(cc1, cc1.n)] if r == 0 else [], inc=sg, k=16)
    p.wait("vector", sg)
    p.emit("vector", lambda v: v.tensor_scalar_mul(
        bcolf, hb4[0], selc[:, 0:1]), inc=sv)
    for r in range(1, 4):
        p.emit("vector", lambda v, r=r: v.scalar_tensor_tensor(
            bcolf, hb4[r], selc[:, r:r + 1], bcolf, ALU.mult, ALU.add),
            waits=[(sv, sv.n)], inc=sv)
    for c in range(NCH):
        p.emit("vector", lambda v, c=c: v.tensor_copy(
            wk0[:, 128 * c:128 * c + 1], bcolf[:, c:c + 1]),
            waits=[(sv, sv.n)], inc=sv)
        p.emit("vector", lambda v, c=c: v.tensor_copy(
            wk0[:, 128 * c + 1:128 * (c + 1)], p.lnx8[c][:, 0:127]), inc=sv)

    p.wait("scalar", sy)
    p.wait("tensor", sy)
    p.wait("tensor", sg)
    p.emit("scalar", lambda x_: x_.activation(
        p.rowA[0:1, 0:16], alogR[0:1, :], AF.Exp), inc=sa)
    p.emit("vector", lambda v: v.tensor_scalar_mul(
        p.rowA[0:1, 0:16], p.rowA[0:1, 0:16], -1.0), waits=[(sa, sa.n)], inc=sv)
    p.emit("vector", lambda v: v.tensor_copy(
        p.rowAb[0:1, 0:16], p.rowA[0:1, 0:16]), waits=[(sv, sv.n)], inc=sv)
    p.emit("tensor", lambda t: t.matmul(
        p.pb[3][:, 0:16], p.ones_row[:, :], p.rowAb[0:1, 0:16],
        start=True, stop=True), waits=[(sv, sv.n)], inc=sp)
    p.emit("vector", lambda v: v.tensor_copy(
        Abc, p.pb[3][:, 0:16]), waits=[(sp, sp.n)], inc=sv)
    p.emit("vector", lambda v: v.memset(gsum, 0.0), waits=[(sv, sv.n)], inc=sv)

    for i in range(16):
        ql = slice(128 * i, 128 * (i + 1))
        ps = p.pb[4]
        wps = [(sv, sv.n), (sa, sa.n)]
        for ic in range(NCH):
            p.emit("tensor", lambda t, ic=ic, ql=ql, ps=ps: t.matmul(
                ps[0:128, 0:48], p.lnx8[ic][:, ql], gbaC[:, 48 * ic:48 * (ic + 1)],
                start=(ic == 0), stop=False),
                waits=wps if ic == 0 else [], inc=None)
        p.emit("tensor", lambda t, ps=ps: t.matmul(
            ps[0:128, 32:48], p.ones_row[0:1, :], dtbR[0:1, :],
            start=False, stop=True), inc=sp)
        p.emit("scalar", lambda x_, i=i, ps=ps: x_.activation(
            gbd[i][:, 0:32], ps[0:128, 0:32], AF.Sigmoid),
            waits=[(sp, sp.n)], inc=sa)
        p.emit("scalar", lambda x_, ps=ps: x_.activation(
            p.scr0[:, 0:16], ps[0:128, 32:48], AF.Abs), inc=sa)
        p.emit("scalar", lambda x_: x_.activation(
            p.scr0[:, 0:16], p.scr0[:, 0:16], AF.Exp, scale=-1.0), inc=sa)
        p.emit("scalar", lambda x_: x_.activation(
            p.scr0[:, 0:16], p.scr0[:, 0:16], AF.Ln, bias=1.0), inc=sa)
        p.emit("vector", lambda v, ps=ps: v.scalar_tensor_tensor(
            p.scr0[:, 16:32], ps[0:128, 32:48], 0.0, p.scr0[:, 0:16],
            ALU.max, ALU.add), waits=[(sa, sa.n)], inc=sv)
        p.emit("vector", lambda v, i=i: v.tensor_mul(
            gbd[i][:, 32:48], p.scr0[:, 16:32], Abc), waits=[(sv, sv.n)], inc=sv)
        p.emit("vector", lambda v, i=i: v.tensor_copy(
            dcb[i], gbd[i][:, 32:48]), waits=[(sv, sv.n)], inc=sv)
        p.emit("tensor", lambda t, i=i: t.matmul(
            p.pb[3][0:128, 0:16], triT, dcb[i], start=True, stop=True),
            waits=[(sv, sv.n)], inc=sp)
        p.emit("vector", lambda v, i=i: v.tensor_copy(
            dcum[i], p.pb[3][0:128, 0:16]), waits=[(sp, sp.n)], inc=sv)
        p.emit("vector", lambda v, i=i: v.tensor_copy(
            dcb[i], dcum[i]), waits=[(sv, sv.n)], inc=sv)
        p.emit("tensor", lambda t, i=i: t.matmul(
            p.pb[3][0:1, 0:16], e127, dcb[i], start=True, stop=True),
            waits=[(sv, sv.n)], inc=sp)
        p.emit("vector", lambda v, i=i: v.tensor_copy(
            offsF[0:1, 16 * i:16 * (i + 1)], gsum), waits=[(sp, sp.n)], inc=sv)
        p.emit("vector", lambda v, i=i: v.tensor_copy(
            offsB[0:1, 16 * i:16 * (i + 1)], gsum), waits=[(sv, sv.n)], inc=sv)
        p.emit("vector", lambda v, i=i: v.tensor_copy(
            dlB[0:1, 16 * i:16 * (i + 1)], p.pb[3][0:1, 0:16]),
            waits=[(sv, sv.n)], inc=sv)
        p.emit("vector", lambda v, i=i: v.tensor_add(
            gsum, gsum, p.pb[3][0:1, 0:16]), waits=[(sv, sv.n)], inc=sv)
    p.barrier()
    if dbg == "gbd":
        _dump((p.x8[0][:, 0:768], bigBf[0:128, 0:768]),
              (p.x8[1][:, 0:256], bigBf[0:128, 768:1024]),
              (p.x8[2][0:1, 0:256], offsF),
              (p.x8[2][0:1, 256:512], dlB),
              (p.x8[2][0:1, 512:528], gsum),
              (p.x8[3][:, 0:1024], wk0))
        return

    gv = ring.load(W["wwB"][0], 2)
    gv2 = ring.load(W["wwB"][1], 2)
    for i in range(16):
        ql = slice(128 * i, 128 * (i + 1))
        for half in range(2):
            g_ = gv if half == 0 else gv2
            ps = p.pb[4 + half]
            for ic in range(NCH):
                p.emit("tensor", lambda t, ic=ic, g_=g_, ql=ql, ps=ps: t.matmul(
                    ps[:, :], p.lnx8[ic][:, ql], ring.view(g_, ic * 512, 512),
                    start=(ic == 0), stop=(ic == NCH - 1)),
                    waits=([(ring.ld, ring.wv(g_)), (sv, sv.n)]) if ic == 0 else [],
                    inc=sp if ic == NCH - 1 else None)
        for h in range(16):
            ps = p.pb[4 + h // 8]
            p.emit("vector", lambda v, i=i, h=h, ps=ps: v.tensor_scalar_mul(
                vwT[:, 1024 * i + 64 * h:1024 * i + 64 * (h + 1)],
                ps[:, 64 * (h % 8):64 * (h % 8 + 1)],
                gbd[i][:, 16 + h:17 + h]), waits=[(sp, sp.n)], inc=sv)
    ring.mark_free(gv, sp, sp.n)
    ring.mark_free(gv2, sp, sp.n)
    p.barrier()
    if dbg == "v":
        _dump(*[(p.x8[c][:, :], vwT[:, 2048 * c:2048 * (c + 1)])
                for c in range(NCH)])
        return

    p.emit("gpsimd", lambda g: g.memset(bigBf[0:128, 1024:1536], 0.0), inc=sg)
    p.wait("vector", sg)
    p.wait("tensor", sg)
    for i in range(16):
        tsl = slice(128 * i, 128 * (i + 1))
        for jc in range(NCH):
            src = (wk0[:, 128 * jc:128 * (jc + 1)] if i == 0
                   else p.lnx8[jc][:, 128 * i - 1:128 * i + 127])
            p.emit("tensor", lambda t, jc=jc, src=src: t.transpose(
                p.pt[0:128, 128 * jc:128 * jc + 128], src, p.ident[:, :]),
                waits=[(sv, sv.n), (sa, sa.n)] if jc == 0 else [], inc=sp)
        wkt_done = sp.n
        for h in range(16):
            jc = h // 2
            hb = (h % 2) * 64
            rk_ap = p.lnx8[jc][hb:hb + 64, tsl]
            wk_ap = (wk0[:, 128 * jc:128 * (jc + 1)][hb:hb + 64, :] if i == 0
                     else p.lnx8[jc][hb:hb + 64, 128 * i - 1:128 * i + 127])
            dcol = dcum[i][:, h:h + 1]
            vsl = vwT[:, 1024 * i + 64 * h:1024 * i + 64 * (h + 1)]
            p.emit("tensor", lambda t, wk_ap=wk_ap, rk_ap=rk_ap: t.matmul(
                p.pb[0][:, 0:128], wk_ap, rk_ap, start=True, stop=True),
                waits=[(sv, sv.n), (sa, sa.n)], inc=sp)
            p.emit("vector", lambda v, dcol=dcol: v.tensor_scalar_mul(
                cbB, onesq, dcol), waits=[(sv, sv.n)], inc=sv)
            p.emit("tensor", lambda t: t.transpose(
                p.pb[1].bitcast(BF16)[:, 0:128], cbB, p.ident[:, :]),
                waits=[(sv, sv.n)], inc=sp)
            p.emit("vector", lambda v, dcol=dcol: v.tensor_scalar_mul(
                cols[:, 0:1], dcol, -1.0), waits=[(sv, sv.n)], inc=sv)
            p.emit("vector", lambda v: v.tensor_add(
                p.scr1[:, 0:128], p.pb[1].bitcast(BF16)[:, 0:128], strictT),
                waits=[(sp, sp.n), (sv, sv.n)], inc=sv)
            p.emit("scalar", lambda x_: x_.activation(
                p.scr1[:, 0:128], p.scr1[:, 0:128], AF.Exp,
                bias=cols[:, 0:1]), waits=[(sv, sv.n)], inc=sa)
            p.emit("vector", lambda v: v.tensor_mul(
                intb, p.scr1[:, 0:128], p.pb[0][:, 0:128]),
                waits=[(sa, sa.n)], inc=sv)
            p.emit("tensor", lambda t, vsl=vsl: t.matmul(
                p.pb[2][:, 0:64], intb, vsl, start=True, stop=True),
                waits=[(sv, sv.n)], inc=sp)
            p.emit("vector", lambda v, h=h, hb=hb: v.tensor_copy(
                Sb[hb:hb + 64, :], S8[h // 2][(h % 2) * 64:(h % 2) * 64 + 64, :]),
                waits=[(sv, sv.n)], inc=sv)
            p.emit("tensor", lambda t, rk_ap=rk_ap, hb=hb: t.matmul(
                p.pb[3][:, 0:64], rk_ap, Sb[hb:hb + 64, :], start=True, stop=True),
                waits=[(sv, sv.n)], inc=sp)
            p.emit("scalar", lambda x_, dcol=dcol: x_.activation(
                cols[:, 1:2], dcol, AF.Exp), waits=[(sa, sa.n)], inc=sa)
            p.emit("tensor", lambda t, i=i, h=h: t.matmul(
                p.pb[4][:, 0:1], p.ones_row[0:1, :],
                dlB[0:1, 16 * i + h:16 * i + h + 1],
                start=True, stop=True), waits=[(sv, sv.n)], inc=sp)
            p.emit("vector", lambda v: v.tensor_copy(
                cols[:, 3:4], p.pb[4][:, 0:1]), waits=[(sp, sp.n)], inc=sv)
            p.emit("scalar", lambda x_, dcol=dcol: x_.activation(
                cols[:, 2:3], dcol, AF.Exp, scale=-1.0, bias=cols[:, 3:4]),
                waits=[(sv, sv.n)], inc=sa)
            p.emit("scalar", lambda x_: x_.activation(
                cols[:, 4:5], cols[:, 3:4], AF.Exp), waits=[(sa, sa.n)], inc=sa)
            p.emit("vector", lambda v, jc=jc, hb=hb: v.tensor_scalar_mul(
                wkts, p.pt[0:128, 128 * jc + hb:128 * jc + hb + 64],
                cols[:, 2:3]), waits=[(sp, wkt_done), (sa, sa.n)], inc=sv)
            p.emit("tensor", lambda t, vsl=vsl, hb=hb: t.matmul(
                p.pb[5][hb:hb + 64, 0:64], wkts, vsl, start=True, stop=True),
                waits=[(sv, sv.n)], inc=sp)
            Ssl = S8[h // 2][(h % 2) * 64:(h % 2) * 64 + 64, :]
            p.emit("vector", lambda v, h=h, Ssl=Ssl: v.tensor_scalar_mul(
                Ssl, Ssl, cols[(h % 2) * 64:(h % 2) * 64 + 64, 4:5]),
                waits=[(sp, sp.n)], inc=sv)
            p.emit("vector", lambda v, Ssl=Ssl, hb=hb: v.tensor_add(
                Ssl, Ssl, p.pb[5][hb:hb + 64, 0:64]), waits=[(sv, sv.n)], inc=sv)
            p.emit("vector", lambda v: v.tensor_copy(
                o64, p.pb[2][:, 0:64]), waits=[(sv, sv.n)], inc=sv)
            p.emit("vector", lambda v, vsl=vsl: v.scalar_tensor_tensor(
                vsl, p.pb[3][:, 0:64], cols[:, 1:2], o64, ALU.mult, ALU.add),
                waits=[(sv, sv.n)], inc=sv)
    p.barrier()
    if dbg == "o1":
        _dump(*[(p.x8[c][:, :], vwT[:, 2048 * c:2048 * (c + 1)])
                for c in range(NCH)])
        return
    if dbg == "s8":
        _dump((p.x8[0][:, 0:512], bigBf[0:128, 1024:1536]),
              (p.x8[1][0:1, 0:16], gsum))
        return

    for t in range(8):
        p.emit("sync", lambda s_, t=t: s_.dma_start(
            out=payS[:, 64 * t:64 * (t + 1)], in_=S8[t]), inc=sy, k=16)
    p.emit("scalar", lambda x_: x_.activation(
        p.rowA[0:1, 0:16], gsum, AF.Exp), inc=sa)
    p.wait("sync", sa)
    p.emit("sync", lambda s_: s_.dma_start(
        out=payS[0:1, 512:528], in_=p.rowA[0:1, 0:16]), inc=sy, k=16)
    p.wait("gpsimd", sy)
    p.emit("gpsimd", lambda g: g.collective_compute(
        "AllGather", ALU.bypass, replica_groups=GROUPS,
        ins=[payS.opt()], outs=[agoS.opt()]), inc=cc2)
    for r in range(3):
        p.emit("gpsimd", lambda g, r=r: g.dma_start(
            out=pfx[r], in_=agoS[128 * r:128 * (r + 1), 0:512]),
            waits=[(cc2, cc2.n)] if r == 0 else [], inc=sg, k=16)
        p.emit("gpsimd", lambda g, r=r: g.dma_start(
            out=dtR16[0:16, r:r + 1], in_=agoS[128 * r:128 * r + 1, 512:528]
            .rearrange("a b -> b a"), allow_slow_non_contiguous=True),
            inc=sg, k=16)
    p.wait("vector", sg)
    p.wait("tensor", sg)
    for r in range(1, 3):
        p.emit("vector", lambda v, r=r: v.tensor_copy(
            dcolB[:, 0:1], dtR16[0:16, r:r + 1]), waits=[(sv, sv.n)], inc=sv)
        for t in range(8):
            p.emit("tensor", lambda t_, t=t: t_.matmul(
                p.pb[4][:, 0:1], eselT[0:16, 128 * t:128 * (t + 1)],
                dcolB[:, 0:1], start=True, stop=True),
                waits=[(sv, sv.n)], inc=sp)
            p.emit("vector", lambda v: v.tensor_copy(
                cols[:, 5:6], p.pb[4][:, 0:1]), waits=[(sp, sp.n)], inc=sv)
            p.emit("vector", lambda v, t=t, r=r: v.scalar_tensor_tensor(
                pfx[r][:, 64 * t:64 * (t + 1)], pfx[r - 1][:, 64 * t:64 * (t + 1)],
                cols[:, 5:6], pfx[r][:, 64 * t:64 * (t + 1)],
                ALU.mult, ALU.add), waits=[(sv, sv.n)], inc=sv)
    p.emit("vector", lambda v: v.tensor_scalar_mul(
        carry, pfx[0], selc[:, 0:1]), waits=[(sv, sv.n)], inc=sv)
    for r in range(1, 3):
        p.emit("vector", lambda v, r=r: v.scalar_tensor_tensor(
            carry, pfx[r], selc[:, r:r + 1], carry, ALU.mult, ALU.add),
            waits=[(sv, sv.n)], inc=sv)
    p.emit("vector", lambda v: v.tensor_copy(carryB, carry),
           waits=[(sv, sv.n)], inc=sv)
    p.barrier()
    if dbg == "carry":
        _dump((p.x8[0][:, 0:512], carry),
              (p.x8[1][:, 0:512], pfx[0]),
              (p.x8[2][:, 0:512], pfx[1]),
              (p.x8[3][:, 0:512], pfx[2]),
              (p.x8[4][0:16, 0:4], dtR16))
        return

    for i in range(16):
        tsl = slice(128 * i, 128 * (i + 1))
        for h in range(16):
            jc = h // 2
            hb = (h % 2) * 64
            rk_ap = p.lnx8[jc][hb:hb + 64, tsl]
            vsl = vwT[:, 1024 * i + 64 * h:1024 * i + 64 * (h + 1)]
            p.emit("tensor", lambda t, i=i, h=h: t.matmul(
                p.pb[4][:, 0:1], p.ones_row[0:1, :],
                offsB[0:1, 16 * i + h:16 * i + h + 1], start=True, stop=True),
                waits=[(sv, sv.n)], inc=sp)
            p.emit("vector", lambda v: v.tensor_copy(
                cols[:, 3:4], p.pb[4][:, 0:1]), waits=[(sp, sp.n)], inc=sv)
            p.emit("scalar", lambda x_, i=i, h=h: x_.activation(
                cols[:, 1:2], dcum[i][:, h:h + 1], AF.Exp, bias=cols[:, 3:4]),
                waits=[(sv, sv.n)], inc=sa)
            p.emit("tensor", lambda t, rk_ap=rk_ap, h=h: t.matmul(
                p.pb[2][:, 0:64], rk_ap,
                carryB[(h % 2) * 64:(h % 2) * 64 + 64,
                       64 * (h // 2):64 * (h // 2) + 64],
                start=True, stop=True), waits=[(sv, sv.n), (sa, sa.n)], inc=sp)
            p.emit("vector", lambda v, vsl=vsl: v.scalar_tensor_tensor(
                vsl, p.pb[2][:, 0:64], cols[:, 1:2], vsl, ALU.mult, ALU.add),
                waits=[(sp, sp.n)], inc=sv)
    p.barrier()
    if dbg == "o2":
        _dump(*[(p.x8[c][:, :], vwT[:, 2048 * c:2048 * (c + 1)])
                for c in range(NCH)])
        return

    gwo1 = ring.load(W["woG"][0], 2)
    gwo2 = ring.load(W["woG"][1], 2)
    for i in range(16):
        tsl = slice(128 * i, 128 * (i + 1))
        for h in range(16):
            p.emit("vector", lambda v, i=i, h=h: v.tensor_scalar_mul(
                vwT[:, 1024 * i + 64 * h:1024 * i + 64 * (h + 1)],
                vwT[:, 1024 * i + 64 * h:1024 * i + 64 * (h + 1)],
                gbd[i][:, h:h + 1]), waits=[(sv, sv.n)] if h == 0 else [],
                inc=sv)
        for j in range(8):
            p.emit("tensor", lambda t, i=i, j=j: t.transpose(
                p.pt[0:128, 128 * j:128 * (j + 1)],
                vwT[:, 1024 * i + 128 * j:1024 * i + 128 * (j + 1)],
                p.ident[:, :]), waits=[(sv, sv.n)] if j == 0 else [], inc=sp)
        p.emit("vector", lambda v: v.tensor_copy(
            ofm, p.pt[0:128, 0:1024]), waits=[(sp, sp.n)], inc=sv)
        sv_pr = sv.n
        for oc in range(NCH):
            g = gwo1 if oc < 4 else gwo2
            bk = oc % 2
            ps = p.pb[bk]
            wfree = [(sv, sv.n)]
            for hdc in range(NCH):
                p.emit("tensor", lambda t, hdc=hdc, g=g, ps=ps, oc=oc: t.matmul(
                    ps[:, 0:128], ring.view(g, (oc % 4) * 1024 + hdc * 128, 128),
                    ofm[:, 128 * hdc:128 * (hdc + 1)],
                    start=(hdc == 0), stop=(hdc == NCH - 1)),
                    waits=([(ring.ld, ring.wv(g))] + wfree) if hdc == 0 else [],
                    inc=sp if hdc == NCH - 1 else None)
            p.emit("vector", lambda v, oc=oc, tsl=tsl, ps=ps: v.tensor_add(
                p.x8[oc][:, tsl], p.x8[oc][:, tsl], ps[:, 0:128]),
                waits=[(sp, sp.n)], inc=sv)
    ring.mark_free(gwo1, sp, sp.n)
    ring.mark_free(gwo2, sp, sp.n)
    for c in range(NCH):
        p.emit("vector", lambda v, c=c: v.tensor_add(
            p.x8[c][:, :], p.x8[c][:, :], p.lnx8[c][:, :]),
            waits=[(sv, sv.n)] if c == 0 else [], inc=sv)
    p.barrier()


def prep_l2_attn(inp):
    Wg = np.asarray(inp["L2_attn_W_gate"], np.float32)
    Wb = np.asarray(inp["L2_attn_W_beta"], np.float32)
    Wa = np.asarray(inp["L2_attn_W_alpha"], np.float32)
    gba = np.concatenate([Wg.T, Wb.T, Wa.T], 1)
    gbaC = np.ascontiguousarray(
        gba.reshape(8, 128, 48).transpose(1, 0, 2).reshape(128, 384))
    WwT = _t(inp["L2_attn_W_write"])
    WoT = _t(inp["L2_attn_W_out"])
    wwC = np.ascontiguousarray(
        WwT.reshape(8, 128, 1024).transpose(1, 0, 2).reshape(128, 8192))
    return {
        "gbaC": gbaC,
        "wwB": np.stack([wwC[:, 0:4096], wwC[:, 4096:8192]])
        if False else np.stack([
            np.ascontiguousarray(WwT[:, 0:512].reshape(8, 128, 512)
                                 .transpose(1, 0, 2).reshape(128, 4096)),
            np.ascontiguousarray(WwT[:, 512:1024].reshape(8, 128, 512)
                                 .transpose(1, 0, 2).reshape(128, 4096))]),
        "woG": host_blocks(WoT, 128).transpose(1, 0, 2).reshape(128, 8192)
        .reshape(128, 2, 4096).transpose(1, 0, 2).copy(),
        "dtbR": np.asarray(inp["L2_attn_dt_bias"], np.float32).reshape(1, 16),
        "alogR": np.asarray(inp["L2_attn_A_log"], np.float32).reshape(1, 16),
    }


def percore_l2_attn(x_full, core):
    b, q = divmod(core, 4)
    sel = np.zeros((128, 4), np.float32)
    if q > 0:
        sel[:, q - 1] = 1.0
    return {"l2sel": sel}


L2ATTN_SHAPES = {
    "gbaC": (128, 384), "wwB": (2, 128, 4096), "woG": (2, 128, 4096),
    "dtbR": (1, 16), "alogR": (1, 16), "l2sel": (128, 4),
}




def _colmajor(v, nch):
    return np.ascontiguousarray(np.asarray(v, np.float32).reshape(nch, 128).T)


def _t(a):
    return np.ascontiguousarray(np.asarray(a, np.float32).T)


def prep_l0_ffn(inp):
    return {
        "w1B": host_blocks(_t(inp["L0_ffn_w1_w"]), 512),
        "w2B": host_blocks(_t(inp["L0_ffn_w2_w"]), 512),
        "h1B": host_blocks(_t(inp["L0_ffn_h1_w"]), 128),
        "h2B": host_blocks(_t(inp["L0_ffn_h2_w"]), 128),
        "w3B": host_blocks(_t(inp["L0_ffn_w3_w"]), 128),
        "s1": _colmajor(inp["L0_ffn_w1_s"], 32),
        "s2": _colmajor(inp["L0_ffn_w2_s"], 32),
        "sh1": _colmajor(inp["L0_ffn_h1_s"], 32),
        "sh2": _colmajor(inp["L0_ffn_h2_s"], 32),
        "s3": _colmajor(inp["L0_ffn_w3_s"], 8),
    }


L0FFN_SHAPES = {
    "w1B": (8, 128, 4096), "w2B": (8, 128, 4096), "h1B": (32, 128, 4096),
    "h2B": (32, 128, 4096), "w3B": (8, 128, 4096),
    "s1": (128, 32), "s2": (128, 32), "sh1": (128, 32), "sh2": (128, 32),
    "s3": (128, 8),
}



def build_layer(wshapes, emit_fn):
    nc = bass.Bass("TRN2", target_bir_lowering=False,
                   dynamic_dma_scratch_size=16384)
    x_in = nc.dram_tensor("x_in", [D, T], F32, kind="ExternalInput").ap()
    x_out = nc.dram_tensor("x_out", [D, T], F32, kind="ExternalOutput").ap()
    W = {k: nc.dram_tensor(k, list(s), F32, kind="ExternalInput").ap()
         for k, s in wshapes.items()}
    with ExitStack() as ctx:
        p = P(nc, ctx)
        setup_common(p)
        load_x(p, x_in)
        emit_fn(p, W)
        store_x(p, x_out)
        p.wait("sync", p.sems["xout"])
        p.run()
    return nc


def shard_x(x_full, core):
    b, q = divmod(core, 4)
    return np.ascontiguousarray(x_full[b, q * T:(q + 1) * T, :].T)


def unshard_x(results, key="x_out"):
    out = np.empty((2, 8192, 1024), np.float32)
    for core in range(8):
        b, q = divmod(core, 4)
        out[b, q * T:(q + 1) * T, :] = results[core][key].T
    return out


def run_layer(nc, W, x_full, percore=None):
    in_maps = []
    for core in range(8):
        m = dict(W)
        m["x_in"] = shard_x(x_full, core)
        if percore is not None:
            m.update(percore(x_full, core))
        in_maps.append(m)
    res = run_bass_kernel_spmd(nc, in_maps, core_ids=list(range(8)))
    return unshard_x(res.results)



_LAYER_SPECS = [
    ("l0a", "L0ATTN", None),
    ("l0f", "L0FFN", None),
    ("l1a", "L1ATTN", None),
    ("l1f", "MOE_L1", None),
    ("l2a", "L2ATTN", None),
    ("l2f", "MOE_L2", None),
    ("l3a", "L3ATTN", None),
    ("l3f", "L3FFN", None),
]

_BUILT_NCS = {}


def _layer_table():
    return [
        (L0ATTN_SHAPES, emit_l0_attn, prep_l0_attn, percore_l0_attn),
        (L0FFN_SHAPES, emit_l0_ffn, prep_l0_ffn, None),
        (L1ATTN_SHAPES, emit_l1_attn, prep_l1_attn, percore_l1_attn),
        (MOE_L1_SHAPES, emit_l1_ffn, prep_moe_l1, None),
        (L2ATTN_SHAPES, emit_l2_attn, prep_l2_attn, percore_l2_attn),
        (MOE_L2_SHAPES, emit_l2_ffn, prep_moe_l2, None),
        (L3ATTN_SHAPES, emit_l3_attn, prep_l3_attn, percore_l3_attn),
        (L3FFN_SHAPES, emit_l3_ffn, prep_l3_ffn, None),
    ]


def kernel(**inputs):
    inputs = {k: np.asarray(v) for k, v in inputs.items()}
    x = np.asarray(inputs["L0_attn_x"], np.float32)
    for li, (shapes, emit_fn, prep_fn, percore) in enumerate(_layer_table()):
        if li not in _BUILT_NCS:
            _BUILT_NCS[li] = build_layer(shapes, emit_fn)
        W = prep_fn(inputs)
        x = run_layer(_BUILT_NCS[li], W, x, percore=percore)
    return x
